# version 09
# baseline (speedup 1.0000x reference)
import math

import numpy as np
import ml_dtypes

import concourse.bass as bass
import concourse.mybir as mybir
from concourse.bass_utils import run_bass_kernel_spmd

F32 = mybir.dt.float32
BF16 = mybir.dt.bfloat16
AF = mybir.ActivationFunctionType
ALU = mybir.AluOpType
AX = mybir.AxisListType

CC = 2048
SET = [[0, 3, 4, 7, 8, 11, 12, 15], [1, 2, 5, 6, 9, 10, 13, 14]]
LPAD = [2, 4, 6, 8, 10, 12, 14, 16]
RSQD = 1.0 / math.sqrt(128.0)
NEG = -1.0e30
DS = 120
GROUPS = [[0, 1], [2, 3], [4, 5], [6, 7]]
ENGS = ("pe", "dv", "ac", "gp", "sp")

N_SUB = 3
DEV_DBG = False
FUSE_COMB = True
EARLY_CC = False


class Plan:
    def __init__(self, nc):
        self.nc = nc
        self.ops = {e: [] for e in ENGS}
        self.sems = {}
        self.counts = {}
        self._lastwait = {}

    def sem(self, name):
        self.sems[name] = self.nc.alloc_semaphore(name)
        self.counts[name] = 0

    def cur(self, name):
        return self.counts[name]

    def emit(self, eng, fn=None, waits=(), inc=()):
        wl = []
        for n, v in waits:
            if v <= 0:
                continue
            k = (eng, n)
            if self._lastwait.get(k, -1) >= v:
                continue
            self._lastwait[k] = v
            wl.append((self.sems[n], v))
        il = []
        for n, k in inc:
            self.counts[n] += k
            il.append((self.sems[n], k))

        def run(h, wl=wl, fn=fn, il=il):
            for s, v in wl:
                h.wait_ge(s, v)
            if fn is not None:
                inst = fn(h)
                for s, k in il:
                    inst.then_inc(s, k)

        self.ops[eng].append(run)


def _build_program(DEV_DBG=None):
    if DEV_DBG is None:
        DEV_DBG = globals()["DEV_DBG"]
    nc = bass.Bass("TRN2", target_bir_lowering=False)
    dt = nc.dram_tensor

    x_in = dt("x", [1024, CC], F32, kind="ExternalInput").ap()
    y_out = dt("y", [1024, CC], F32, kind="ExternalOutput").ap()
    wa_d = dt("wa", [CC, 6144], F32, kind="ExternalInput").ap()
    wp_d = dt("wp", [CC, CC], F32, kind="ExternalInput").ap()
    wq_d = dt("wq", [CC, 4096], F32, kind="ExternalInput").ap()
    wk_d = dt("wk", [CC, 1024], F32, kind="ExternalInput").ap()
    wv_d = dt("wv", [CC, 512], F32, kind="ExternalInput").ap()
    wo_d = dt("wo", [CC, CC], F32, kind="ExternalInput").ap()
    wstd_d = dt("wstd", [16, 1], F32, kind="ExternalInput").ap()
    wrec_d = dt("wrec", [16, 1], F32, kind="ExternalInput").ap()
    rmsw_d = dt("rmsw", [128, 1], F32, kind="ExternalInput").ap()
    lq1_d = dt("lq1", [16, 128], F32, kind="ExternalInput").ap()
    lq2_d = dt("lq2", [16, 128], F32, kind="ExternalInput").ap()
    lk1_d = dt("lk1", [16, 128], F32, kind="ExternalInput").ap()
    lk2_d = dt("lk2", [16, 128], F32, kind="ExternalInput").ap()
    ident_d = dt("ident", [128, 128], BF16, kind="ExternalInput").ap()
    maskS_d = dt("maskS", [1024, 256], BF16, kind="ExternalInput").ap()
    sel16_d = dt("sel16", [16, 128], BF16, kind="ExternalInput").ap()
    cs_d = [dt(n, [64, 1024], BF16, kind="ExternalInput").ap()
            for n in ("cos0", "sin0", "cos1", "sin1")]

    colld = []
    for li, (krows, vrows, vcols) in enumerate(
            [(2048, 1024, 2048), (1024, 1024, 512), (2048, 1024, 2048)]):
        kl = dt(f"k{li}l", [krows, 1024], BF16)
        kg = dt(f"k{li}g", [2 * krows, 1024], BF16)
        vl = dt(f"v{li}l", [vrows, vcols], BF16)
        vg = dt(f"v{li}g", [2 * vrows, vcols], BF16)
        colld.append((kl, kg, vl, vg))

    sb = nc.alloc_sbuf_tensor
    xs = [sb(f"xs{j}", [128, CC], F32) for j in range(8)]
    lnf = sb("lnf", [128, 16384], BF16)
    attf = sb("attf", [128, 16384], BF16)
    AR = [sb(f"ar{i}", [128, 2048], BF16) for i in range(7)]
    WH = [sb(f"wh{i}", [128, 2048], BF16) for i in range(2)]
    WB = [sb(f"wb{i}", [128, 4096], BF16) for i in range(2)]
    masksb = sb("masksb", [128, 2048], BF16)
    identb = sb("identb", [128, 128], BF16)
    cosT = [sb("cosT0", [64, 1024], BF16), sb("cosT1", [64, 1024], BF16)]
    sinT = [sb("sinT0", [64, 1024], BF16), sb("sinT1", [64, 1024], BF16)]
    onesc = sb("onesc", [128, 1], BF16)
    onesr = sb("onesr", [1, 128], BF16)
    qt_sb = sb("qt_sb", [128, 1024], BF16)
    STG = [sb(f"stg{i}", [128, 1024], BF16) for i in range(2)]
    sqsm = sb("sqsm", [128, 512], BF16)
    avstg = [sb(f"avs{i}", [128, 128], BF16) for i in range(2)]
    rr = {"im": AR[5][0:64, 0:1024], "t1": AR[5][0:64, 1024:2048],
          "t2": AR[6][0:64, 0:1024]}
    kcolbf = sb("kcolbf", [128, 1], BF16)
    knrow = sb("knrow", [1, 128], F32)
    knmax = sb("knmax", [1, 1], F32)
    knmax_bf = sb("knmax_bf", [1, 1], BF16)
    rowsc = sb("rowsc", [1, 16], BF16)
    sel16 = sb("sel16s", [16, 128], BF16)
    t16a = sb("t16a", [16, 256], F32)
    t16b = sb("t16b", [16, 8], F32)
    t16bf = sb("t16bf", [16, 4], BF16)

    sf = lambda n, c: sb(n, [128, c], F32)
    lnsum = sf("lnsum", 8); lnsq = sf("lnsq", 8); lnmean = sf("lnmean", 8)
    lnvar = sf("lnvar", 8); lnms = sf("lnms", 8); lnrstd = sf("lnrstd", 8)
    epsln = sf("epsln", 1); zcol = sf("zcol", 1); negone = sf("negone", 1)
    scstd = sf("scstd", 16); sclow = sf("sclow", 1)
    ulow = sf("ulow", 1); negu = sf("negu", 1)
    lam_b = sf("lam_b", 16); rms08 = sf("rms08", 1)
    neglam = sf("neglam", 16); gamc = sf("gamc", 1); rsA = sf("rsA", 2)
    qn2 = sf("qn2", 16); qns = sf("qns", 16)
    sqk = sf("sqk", 1); sqkb = sf("sqkb", 1)
    scol = sf("scol", 8); kn2 = sf("kn2", 1)
    ssum = sf("ssum", 1); ssum2 = sf("ssum2", 1)
    rs = sf("rs", 1); rs2 = sf("rs2", 1); rsP = sf("rsP", 2); s2P = sf("s2P", 2)
    btile = sf("btile", 1); btile2 = sf("btile2", 1)
    btileP = sf("btileP", 2); btile2P = sf("btile2P", 2)
    m2c = sf("m2c", 1); s2sq = sf("s2sq", 1); den = sf("den", 1); rstdc = sf("rstdc", 1)
    beta = sf("beta", 1)

    psS = [nc.alloc_psum_tensor("psA", [128, 512], F32),
           nc.alloc_psum_tensor("psB", [128, 512], F32)]
    trpX = [nc.alloc_psum_tensor("trpA", [128, 1024], BF16),
            nc.alloc_psum_tensor("trpB", [128, 1024], BF16)]
    qnp = nc.alloc_psum_tensor("qnp", [128, 512], F32)
    avp = nc.alloc_psum_tensor("avp", [128, 256], F32)
    mm0 = nc.alloc_psum_tensor("mm0", [128, 512], F32)
    colp = qnp

    P = Plan(nc)
    for s_ in ["pe", "dv", "ac", "ld", "wd", "kv", "st"] + [f"cc{i}" for i in range(6)]:
        P.sem(s_)

    REL = {}

    def rel(name):
        r = REL.get(name)
        return [r] if r else []

    def dv(fn, waits=(), inc=1):
        P.emit("dv", fn, waits=waits, inc=[("dv", inc)] if inc else [])

    def ac(fn, waits=(), inc=1):
        P.emit("ac", fn, waits=waits, inc=[("ac", inc)] if inc else [])

    def pe(fn, waits=(), inc=1):
        P.emit("pe", fn, waits=waits, inc=[("pe", inc)] if inc else [])

    def sp_dma(dst, src, waits=()):
        P.emit("sp", lambda h, dst=dst, src=src: h.dma_start(out=dst, in_=src),
               waits=waits, inc=[("ld", 16)])

    def gp_dma(dst, src, waits=(), sem="wd"):
        P.emit("gp", lambda h, dst=dst, src=src: h.dma_start(out=dst, in_=src),
               waits=waits, inc=[(sem, 16)])

    dbg_done = set()

    def dbg(name, ap, shape, dtype, waits):
        if not DEV_DBG or name in dbg_done:
            return
        dbg_done.add(name)
        dd = dt("dbg_" + name, list(shape), dtype, kind="ExternalOutput").ap()
        sp_dma(dd[tuple(slice(None) for _ in shape)], ap, waits=waits)

    for j in range(8):
        sp_dma(xs[j][:, :], x_in[j * 128:(j + 1) * 128, :])
    x_ld = [("ld", 16 * (j + 1)) for j in range(8)]
    sp_dma(identb[:, :], ident_d[:, :])
    sp_dma(masksb.ap().rearrange("p (s c) -> p s c", s=8),
           maskS_d.rearrange("(s p) c -> p s c", s=8))
    for a in range(2):
        sp_dma(cosT[a][:, :], cs_d[2 * a][:, :])
        sp_dma(sinT[a][:, :], cs_d[2 * a + 1][:, :])
    sp_dma(t16a[:, 0:128], lq1_d)
    sp_dma(t16a[:, 128:256], lk1_d)
    sp_dma(t16b[:, 0:1], wstd_d)
    sp_dma(t16b[:, 1:2], wrec_d)
    sp_dma(rms08[:, :], rmsw_d)
    sp_dma(sel16[:, :], sel16_d)
    ld_all = [("ld", P.cur("ld"))]

    for ap_, v in [(epsln, 1e-5), (zcol, 0.0), (negone, -1.0)]:
        dv(lambda h, ap_=ap_, v=v: h.memset(ap_[:, :], v), inc=0)
    dv(lambda h: h.memset(onesc[:, :], 1.0), inc=0)
    dv(lambda h: h.memset(onesr[:, :], 1.0), inc=1)

    def edge(eng):
        return [(eng, P.cur(eng))]

    dv(lambda h: h.tensor_scalar(out=rms08[:, :], in0=rms08[:, :], scalar1=0.8,
                                 scalar2=None, op0=ALU.mult), waits=ld_all)
    dv(lambda h: h.tensor_scalar(out=t16b[:, 2:3], in0=t16b[:, 0:1], scalar1=1e-8,
                                 scalar2=None, op0=ALU.max), waits=ld_all)
    dv(lambda h: h.tensor_scalar(out=t16b[:, 3:4], in0=t16b[:, 1:2], scalar1=1e-8,
                                 scalar2=None, op0=ALU.max))
    w = edge("dv")
    for col in (2, 3):
        ac(lambda h, col=col: h.activation(out=t16b[:, col:col + 1], in_=t16b[:, col:col + 1],
                                           func=AF.Ln, bias=zcol[0:16, :], scale=1.0),
           waits=w + edge("ac"))
        w = []
        ac(lambda h, col=col: h.activation(out=t16b[:, col:col + 1], in_=t16b[:, col:col + 1],
                                           func=AF.Exp, bias=zcol[0:16, :], scale=0.5),
           waits=edge("ac"))
    dv(lambda h: h.tensor_scalar(out=t16b[:, 4:5], in0=t16b[:, 1:2], scalar1=0.1,
                                 scalar2=None, op0=ALU.is_gt), waits=ld_all)
    wv_ = edge("ac") + edge("dv")
    dv(lambda h: h.tensor_copy(out=t16bf[:, 0:1], in_=t16b[:, 2:3]), waits=wv_)
    dv(lambda h: h.tensor_copy(out=t16bf[:, 1:2], in_=t16b[:, 3:4]))
    dv(lambda h: h.tensor_copy(out=t16bf[:, 2:3], in_=t16b[:, 4:5]))
    w = edge("dv")
    pe(lambda h: h.transpose(out=trpX[0][0:1, 0:16], in_=t16bf[:, 0:1],
                             identity=identb[0:16, 0:16]), waits=w + ld_all)
    w = edge("pe")
    dv(lambda h: h.tensor_copy(out=rowsc[:, :], in_=trpX[0][0:1, 0:16]), waits=w)
    w = edge("dv")
    pe(lambda h: h.matmul(out=colp[0:120, 0:16], lhsT=onesr[:, 0:120], rhs=rowsc[:, :],
                          start=True, stop=True), waits=w)
    w = edge("pe")
    dv(lambda h: h.tensor_copy(out=scstd[0:120, :], in_=colp[0:120, 0:16]), waits=w)
    w = edge("dv")
    pe(lambda h: h.matmul(out=colp[:, 16:17], lhsT=sel16[:, :], rhs=t16bf[:, 1:2],
                          start=True, stop=True), waits=w)
    pe(lambda h: h.matmul(out=colp[:, 17:18], lhsT=sel16[:, :], rhs=t16bf[:, 2:3],
                          start=True, stop=True))
    w = edge("pe")
    dv(lambda h: h.tensor_copy(out=sclow[:, :], in_=colp[:, 16:17]), waits=w)
    dv(lambda h: h.tensor_copy(out=ulow[:, :], in_=colp[:, 17:18]))
    w = edge("dv")
    dv(lambda h: h.tensor_scalar(out=negu[:, :], in0=ulow[:, :], scalar1=-1.0,
                                 scalar2=None, op0=ALU.mult), waits=w)
    dv(lambda h: h.tensor_tensor(out=t16a[:, 0:128], in0=t16a[:, 0:128], in1=t16a[:, 128:256],
                                 op=ALU.mult), waits=ld_all)
    w = edge("dv")
    dv(lambda h: h.reduce_sum(out=t16b[:, 5:6], in_=t16a[:, 0:128], axis=AX.X), waits=w)
    w = edge("dv")
    sp_dma(t16a[:, 0:128], lq2_d, waits=w)
    sp_dma(t16a[:, 128:256], lk2_d)
    ld2 = [("ld", P.cur("ld"))]
    dv(lambda h: h.tensor_tensor(out=t16a[:, 0:128], in0=t16a[:, 0:128], in1=t16a[:, 128:256],
                                 op=ALU.mult), waits=ld2)
    w = edge("dv")
    dv(lambda h: h.reduce_sum(out=t16b[:, 6:7], in_=t16a[:, 0:128], axis=AX.X), waits=w)
    w = edge("dv")
    ac(lambda h: h.activation(out=t16b[:, 5:6], in_=t16b[:, 5:6], func=AF.Exp,
                              bias=zcol[0:16, :], scale=1.0), waits=w + edge("ac"))
    ac(lambda h: h.activation(out=t16b[:, 6:7], in_=t16b[:, 6:7], func=AF.Exp,
                              bias=zcol[0:16, :], scale=1.0), waits=edge("ac"))
    w = edge("ac")
    dv(lambda h: h.tensor_tensor(out=t16b[:, 5:6], in0=t16b[:, 5:6], in1=t16b[:, 6:7],
                                 op=ALU.subtract), waits=w)
    w = edge("dv")
    dv(lambda h: h.tensor_scalar(out=t16b[:, 5:6], in0=t16b[:, 5:6], scalar1=0.2,
                                 scalar2=None, op0=ALU.add), waits=w)
    w = edge("dv")
    dv(lambda h: h.tensor_copy(out=t16bf[:, 3:4], in_=t16b[:, 5:6]), waits=w)
    w = edge("dv")
    pe(lambda h: h.transpose(out=trpX[0][0:1, 32:48], in_=t16bf[:, 3:4],
                             identity=identb[0:16, 0:16]), waits=w)
    w = edge("pe")
    dv(lambda h: h.tensor_copy(out=rowsc[:, :], in_=trpX[0][0:1, 32:48]), waits=w)
    w = edge("dv")
    pe(lambda h: h.matmul(out=colp[:, 18:34], lhsT=onesr[:, :], rhs=rowsc[:, :],
                          start=True, stop=True), waits=w)
    w = edge("pe")
    dv(lambda h: h.tensor_copy(out=lam_b[:, :], in_=colp[:, 18:34]), waits=w)
    for _t in ("qnp0", "qnp8", "qnp16", "qnp24", "qnpk64", "qnpk80"):
        REL[_t] = ("dv", P.cur("dv"))
    w = edge("dv")
    dv(lambda h: h.tensor_scalar(out=neglam[:, :], in0=lam_b[:, :], scalar1=-1.0,
                                 scalar2=None, op0=ALU.mult), waits=w)
    REL["trp0"] = ("dv", P.cur("dv"))
    REL["trp1"] = ("dv", P.cur("dv"))

    wd_n = [0]

    def wunit(kind, i, dst, src):
        slot = f"{kind}{i}"
        gp_dma(dst, src, waits=rel(slot), sem="wd")
        wd_n[0] += 1
        return wd_n[0] * 16

    wa_r = wa_d.rearrange("(c p) f -> p c f", c=16)
    wp_r = wp_d.rearrange("(c p) f -> p c f", c=16)
    wq_r = wq_d.rearrange("(c p) f -> p c f", c=16)
    wk_r = wk_d.rearrange("(c p) f -> p c f", c=16)
    wv_r = wv_d.rearrange("(c p) f -> p c f", c=16)
    wo_r = wo_d.rearrange("(c p) f -> p c f", c=16)

    tr_i = [0]

    def transpose_batch(tiles, dests, waits, evac=None):
        half = tr_i[0] % 2
        tr_i[0] += 1
        trp = trpX[half]
        wt = list(waits) + rel(f"trp{half}")
        n = len(tiles)
        for t, tile in enumerate(tiles):
            pe(lambda h, tile=tile, t=t, trp=trp: h.transpose(
                out=trp[:, t * 128:(t + 1) * 128], in_=tile, identity=identb[:, :]),
                waits=wt if t == 0 else (), inc=1 if t == n - 1 else 0)
        w2 = edge("pe")
        nd = len(dests)
        for t, d in enumerate(dests):
            src = trp[:, t * 128:(t + 1) * 128] if nd > 1 else trp[:, 0: n * 128]
            if evac is None:
                dv(lambda h, d=d, src=src: h.tensor_copy(out=d, in_=src),
                   waits=w2 if t == 0 else (), inc=1 if t == nd - 1 else 0)
            else:
                dv(lambda h, d=d, src=src: evac(h, d, src),
                   waits=w2 if t == 0 else (), inc=1 if t == nd - 1 else 0)
        REL[f"trp{half}"] = ("dv", P.cur("dv"))

    def lnfc(c):
        return lnf[:, c * 1024:(c + 1) * 1024]

    def build_ln(xdep):
        sqd = AR[2]
        for j in range(8):
            wj = [xdep[j]] + (rel("ar2") if j == 0 else [])
            dv(lambda h, j=j: h.reduce_sum(out=lnsum[:, j:j + 1], in_=xs[j][:, :], axis=AX.X),
               waits=wj, inc=1 if j == 7 else 0)
            ac(lambda h, j=j, sqd=sqd: h.activation(out=sqd[:, :], in_=xs[j][:, :],
                                                    func=AF.Square,
                                                    accum_out=lnsq[:, j:j + 1]),
               waits=wj, inc=1 if j == 7 else 0)
        REL["ar2"] = ("ac", P.cur("ac"))
        w = edge("dv") + edge("ac")
        dv(lambda h: h.tensor_scalar(out=lnmean[:, :], in0=lnsum[:, :], scalar1=1.0 / CC,
                                     scalar2=None, op0=ALU.mult), waits=w)
        w = edge("dv")
        dv(lambda h: h.tensor_tensor(out=lnms[:, :], in0=lnmean[:, :], in1=lnmean[:, :],
                                     op=ALU.mult), waits=w)
        w = edge("dv")
        dv(lambda h: h.scalar_tensor_tensor(out=lnvar[:, :], in0=lnsq[:, :], scalar=1.0 / CC,
                                            in1=lnms[:, :], op0=ALU.mult, op1=ALU.subtract),
           waits=w)
        w = edge("dv")
        ac(lambda h: h.activation(out=lnrstd[:, :], in_=lnvar[:, :], func=AF.Ln,
                                  bias=epsln[:, :], scale=1.0), waits=w)
        w = edge("ac")
        ac(lambda h: h.activation(out=lnrstd[:, :], in_=lnrstd[:, :], func=AF.Exp,
                                  bias=zcol[:, :], scale=-0.5), waits=w)
        w_ac = edge("ac")
        dbg("lnstats", lnsum[:, :], (128, 8), F32, w_ac + edge("dv"))
        dbg("lnsq", lnsq[:, :], (128, 8), F32, w_ac)
        dbg("lnmean", lnmean[:, :], (128, 8), F32, w_ac)
        dbg("lnvar", lnvar[:, :], (128, 8), F32, w_ac)
        dbg("lnrstd", lnrstd[:, :], (128, 8), F32, w_ac)
        for j in range(8):
            lnt = AR[j % 2]
            wj = w_ac + rel(f"ar{j % 2}")
            dv(lambda h, j=j, lnt=lnt: h.tensor_scalar(
                out=lnt[:, :], in0=xs[j][:, :], scalar1=lnmean[:, j:j + 1],
                scalar2=lnrstd[:, j:j + 1], op0=ALU.subtract, op1=ALU.mult), waits=wj)
            w_dvj = edge("dv")
            lnf3 = lnf.ap().rearrange("p (c t) -> p c t", c=16)
            for cb in range(4):
                tiles = [lnt[:, (cb * 4 + t) * 128:(cb * 4 + t + 1) * 128] for t in range(4)]
                dst = lnf3[:, cb * 4:cb * 4 + 4, j * 128:(j + 1) * 128]
                transpose_batch(tiles, [dst], w_dvj + rel("lnf_w") if cb == 0 else rel("lnf_w"))
            REL[f"ar{j % 2}"] = ("pe", P.cur("pe"))
        REL["lnf"] = ("dv", P.cur("dv"))

    def proj_group(psum_ap, lhs_fn, rhs_fn, wthr, extra=()):
        waits = [("wd", wthr)] + list(extra)
        for c in range(16):
            lhs, rhs = lhs_fn(c), rhs_fn(c)
            pe(lambda h, lhs=lhs, rhs=rhs, c=c: h.matmul(
                out=psum_ap, lhsT=lhs, rhs=rhs, start=(c == 0), stop=(c == 15)),
                waits=waits if c == 0 else (), inc=1 if c == 15 else 0)

    def collective(kl, kg, sem_name, wait_ld):
        P.emit("gp", lambda h, kl=kl, kg=kg: h.collective_compute(
            "AllGather", ALU.bypass, replica_groups=GROUPS,
            ins=[kl.ap().opt()], outs=[kg.ap().opt()]),
            waits=[("ld", wait_ld)], inc=[(sem_name, 1)])

    def gather_k(dstT, kgap, rowbase, cwait):
        kg2 = kgap.rearrange("(s r) t -> s r t", s=2)
        for sec in range(2):
            src_rows = kg2[sec, rowbase:rowbase + 128, :]
            for par in range(2):
                dblk = [(0, 3), (1, 2)][sec][par]
                dst = dstT.ap().rearrange("d (a b) -> d a b", a=4)[
                    :, :, dblk * 128:(dblk + 1) * 128]
                src = src_rows.rearrange("r (a q p) -> r a q p", a=4, q=2)[:, :, par, :]
                sp_dma(dst, src, waits=cwait)

    def gather_v(dstT, vgap, colbase, cwait):
        vg5 = vgap.rearrange("(s a q p) f -> p s a q f", s=2, a=4, q=2, p=128)
        for sec in range(2):
            for par in range(2):
                dblk = [(0, 3), (1, 2)][sec][par]
                dst = dstT.ap().rearrange("p (a b) -> p a b", a=4)[
                    :, :, dblk * 128:(dblk + 1) * 128]
                src = vg5[:, sec, :, par, colbase:colbase + 128]
                sp_dma(dst, src, waits=cwait)

    def knorm_chain(Kg_s, kv_ld_w, sqk_dst, kb):
        tk = f"qnpk{kb}"
        for ch in range(4):
            ac(lambda h, Kg_s=Kg_s, ch=ch: h.activation(
                out=sqsm[:, :], in_=Kg_s[:, ch * 512:(ch + 1) * 512], func=AF.Square),
                waits=(kv_ld_w + rel("sqsm") + rel(tk)) if ch == 0 else edge("ac"))
            w2 = edge("ac") + (rel(tk) if ch == 0 else [])
            for t in range(4):
                kc = kb + ch * 4 + t
                pe(lambda h, kc=kc, t=t: h.matmul(
                    out=qnp[:, kc:kc + 1], lhsT=sqsm[:, t * 128:(t + 1) * 128],
                    rhs=onesc[:, :], start=True, stop=True),
                    waits=w2 if t == 0 else (), inc=1 if t == 3 else 0)
            REL["sqsm"] = ("pe", P.cur("pe"))
        w = edge("pe")
        dv(lambda h, kb=kb: h.reduce_max(out=kn2[:, :], in_=qnp[:, kb:kb + 16], axis=AX.X),
           waits=w)
        w = edge("dv")
        dv(lambda h: h.tensor_copy(out=kcolbf[:, :], in_=kn2[:, :]), waits=w)
        w = edge("dv")
        pe(lambda h: h.transpose(out=trpX[0][0:1, 0:128], in_=kcolbf[:, :],
                                 identity=identb[:, :]), waits=w + rel("trp0"))
        w = edge("pe")
        dv(lambda h: h.tensor_copy(out=knrow[:, :], in_=trpX[0][0:1, 0:128]), waits=w)
        REL["trp0"] = ("dv", P.cur("dv"))
        w = edge("dv")
        dv(lambda h: h.reduce_max(out=knmax[:, :], in_=knrow[:, :], axis=AX.X), waits=w)
        w = edge("dv")
        ac(lambda h: h.activation(out=knmax[:, :], in_=knmax[:, :], func=AF.Ln,
                                  bias=epsln[0:1, :], scale=1.0), waits=w + edge("ac"))
        w = edge("ac")
        ac(lambda h: h.activation(out=knmax[:, :], in_=knmax[:, :], func=AF.Exp,
                                  bias=zcol[0:1, :], scale=0.5), waits=w)
        w = edge("ac")
        dv(lambda h: h.tensor_copy(out=knmax_bf[:, :], in_=knmax[:, :]), waits=w)
        w = edge("dv")
        pe(lambda h, kb=kb: h.matmul(out=qnp[:, kb + 32:kb + 33], lhsT=onesr[:, :],
                                     rhs=knmax_bf[:, :], start=True, stop=True), waits=w)
        w = edge("pe")
        dv(lambda h, sqk_dst=sqk_dst, kb=kb: h.tensor_copy(out=sqk_dst[:, :],
                                                           in_=qnp[:, kb + 32:kb + 33]),
           waits=w)
        REL[tk] = ("dv", P.cur("dv"))

    avp_par = [0]

    def attn_tailB(j, Psrc, Vg_s, dco, w_exp, scal_fn, final_scale, vwait=()):
        PT = AR[6]
        ntb = (LPAD[j] + 3) // 4
        for tb in range(ntb):
            nt = min(4, LPAD[j] - tb * 4)
            tiles = [Psrc[:, (tb * 4 + t) * 128:(tb * 4 + t + 1) * 128] for t in range(nt)]
            dsts = [PT[:, (tb * 4 + t) * 128:(tb * 4 + t + 1) * 128] for t in range(nt)]
            transpose_batch(tiles, dsts, (list(w_exp) + rel("ar6")) if tb == 0 else [])
        relP = ("pe", P.cur("pe"))
        w_pt = edge("dv")
        par = avp_par[0] % 2
        avp_par[0] += 1
        avt = avp[:, par * 128:(par + 1) * 128]
        for kt in range(LPAD[j]):
            lhs = PT[:, kt * 128:(kt + 1) * 128]
            rhsap = Vg_s[:, kt * 128:(kt + 1) * 128]
            pe(lambda h, lhs=lhs, rhsap=rhsap, kt=kt, j=j, avt=avt: h.matmul(
                out=avt, lhsT=lhs, rhs=rhsap, start=(kt == 0), stop=(kt == LPAD[j] - 1)),
                waits=(w_pt + rel(f"avt{par}") + list(vwait)) if kt == 0 else ())
        REL["ar6"] = ("pe", P.cur("pe"))
        w_av = edge("pe")
        scal_ap, w_scal = scal_fn(w_av, avt)
        stgt = avstg[par]
        dv(lambda h, stgt=stgt, scal_ap=scal_ap, avt=avt: h.tensor_scalar(
            out=stgt[:, :], in0=avt, scalar1=scal_ap, scalar2=None, op0=ALU.mult),
            waits=w_av + list(w_scal) + rel(f"avstg{par}"))
        REL[f"avt{par}"] = ("dv", P.cur("dv"))
        w2 = edge("dv")
        if final_scale is None:
            transpose_batch([stgt[:, :]], [dco], w2 + rel("attf_w"))
        else:
            transpose_batch([stgt[:, :]], [dco], w2 + rel("attf_w"),
                            evac=lambda h, d, s, fs=final_scale: h.tensor_scalar(
                                out=d, in0=s, scalar1=fs, scalar2=None, op0=ALU.mult))
        REL[f"avstg{par}"] = ("pe", P.cur("pe"))
        return relP

    chpar = [0]

    def score_chunks(j, lhsT_ap, Kgx, Pdst, bt_ap, bt_dv, scolc, wqk, slot_rel_keys):
        Lp = LPAD[j] * 128
        nch = (Lp + 511) // 512
        for ch in range(nch):
            par = chpar[0] % 2
            chpar[0] += 1
            pss = psS[par]
            c0 = ch * 512
            cw = min(512, Lp - c0)
            hasmask = (c0 + cw == Lp)
            wfirst = list(wqk) + rel(f"psS{par}")
            pe(lambda h, pss=pss, cw=cw, lhsT_ap=lhsT_ap, c0=c0, hm=hasmask:
               h.matmul(out=pss[:, 0:cw], lhsT=lhsT_ap, rhs=Kgx[:, c0:c0 + cw],
                        start=True, stop=not hm),
                waits=wfirst, inc=0 if hasmask else 1)
            if hasmask:
                moff = cw - 256
                pe(lambda h, pss=pss, moff=moff, j=j: h.matmul(
                    out=pss[:, moff:moff + 256], lhsT=identb[:, :],
                    rhs=masksb[:, j * 256:(j + 1) * 256], start=False, stop=True))
            w2 = edge("pe")
            ac(lambda h, cw=cw, c0=c0, pss=pss, Pdst=Pdst, bt_ap=bt_ap, ch=ch, scolc=scolc:
               h.activation(out=Pdst[:, c0:c0 + cw], in_=pss[:, 0:cw], func=AF.Exp,
                            bias=bt_ap, scale=RSQD,
                            accum_out=scol[:, scolc + ch:scolc + ch + 1]),
                waits=w2 + list(bt_dv) +
                (([rel(k)[0] for k in slot_rel_keys if REL.get(k)] + rel("scol"))
                 if ch == 0 else []))
            REL[f"psS{par}"] = ("ac", P.cur("ac"))
        return nch

    def qnorm_chain(Qsrc, width, qcols, qld_w):
        nch = width // 512
        q0 = qcols[0]
        tq = f"qnp{q0}"
        for ch in range(nch):
            ac(lambda h, Qsrc=Qsrc, ch=ch: h.activation(
                out=sqsm[:, :], in_=Qsrc[:, ch * 512:(ch + 1) * 512], func=AF.Square),
                waits=(qld_w + rel("sqsm") + rel(tq)) if ch == 0 else edge("ac"))
            w2 = edge("ac") + (rel(tq) if ch == 0 else [])
            for t in range(4):
                cc_ = q0 + ch * 4 + t
                pe(lambda h, cc_=cc_, t=t: h.matmul(
                    out=qnp[:, cc_:cc_ + 1], lhsT=sqsm[:, t * 128:(t + 1) * 128],
                    rhs=onesc[:, :], start=True, stop=True),
                    waits=w2 if t == 0 else (), inc=1 if t == 3 else 0)
            REL["sqsm"] = ("pe", P.cur("pe"))
        w = edge("pe")
        c0, c1 = qcols[0], qcols[0] + width // 128
        dv(lambda h, c0=c0, c1=c1: h.tensor_copy(out=qn2[:, c0:c1], in_=qnp[:, c0:c1]), waits=w)
        REL[tq] = ("dv", P.cur("dv"))
        w = edge("dv")
        ac(lambda h, c0=c0, c1=c1: h.activation(out=qns[:, c0:c1], in_=qn2[:, c0:c1],
                                                func=AF.Ln, bias=epsln[:, :], scale=1.0),
           waits=w + edge("ac"))
        w = edge("ac")
        ac(lambda h, c0=c0, c1=c1: h.activation(out=qns[:, c0:c1], in_=qns[:, c0:c1],
                                                func=AF.Exp, bias=zcol[:, :], scale=0.5),
           waits=w)

    def build_ffn(li, xdep):
        kl, kg, vl, vg = colld[li]
        ccK, ccV = f"cc{li * 2}", f"cc{li * 2 + 1}"
        build_ln(xdep)
        lnf_rel = [REL["lnf"]]
        dbg("lnf", lnf[:, :], (128, 16384), BF16, [REL["lnf"]])

        for h_ in range(16):
            si = h_ % 2
            thr = wunit("wh", si,
                        WH[si][:, 0:1920].rearrange("p (c f) -> p c f", c=16),
                        wa_r[:, :, 1920 + h_ * DS: 1920 + (h_ + 1) * DS])
            kst = STG[si]
            scl = scstd[0:120, h_:h_ + 1]
            for th in range(2):
                psu = mm0[0:120, 0:512]
                proj_group(psu,
                           lambda c, si=si: WH[si][:, c * DS:(c + 1) * DS],
                           lambda c, th=th: lnfc(c)[:, th * 512:(th + 1) * 512],
                           thr, extra=lnf_rel + rel("mm0") + (rel(f"stg{si}") if th == 0 else []))
                w2 = edge("pe")
                ac(lambda h, kst=kst, th=th, psu=psu, scl=scl: h.activation(
                    out=kst[0:120, th * 512:(th + 1) * 512], in_=psu, func=AF.Copy,
                    bias=0.0, scale=scl), waits=w2)
                REL["mm0"] = ("ac", P.cur("ac"))
            REL[f"wh{si}"] = ("pe", P.cur("pe"))
            w2 = edge("ac")
            sp_dma(kl.ap()[h_ * 128: h_ * 128 + 120, :], kst[0:120, :], waits=w2)
            REL[f"stg{si}"] = ("ld", P.cur("ld"))
        qlo, klo, dlo = STG[0], STG[1], AR[4][:, 0:1024]
        for part, dst in [(0, qlo), (1, klo)]:
            si = part
            thr = wunit("wh", si, WH[si][:, 0:2048].rearrange("p (c f) -> p c f", c=16),
                        wa_r[:, :, 5888 + part * 128: 5888 + (part + 1) * 128])
            for th in range(2):
                proj_group(mm0[:, 0:512],
                           lambda c, si=si: WH[si][:, c * 128:(c + 1) * 128],
                           lambda c, th=th: lnfc(c)[:, th * 512:(th + 1) * 512],
                           thr, extra=lnf_rel + rel("mm0") +
                           (rel(f"stg{part}") if th == 0 else []))
                w2 = edge("pe")
                ac(lambda h, dst=dst, th=th: h.activation(
                    out=dst[:, th * 512:(th + 1) * 512], in_=mm0[:, 0:512], func=AF.Copy,
                    bias=0.0, scale=sclow[:, :]), waits=w2)
                REL["mm0"] = ("ac", P.cur("ac"))
            REL[f"wh{si}"] = ("pe", P.cur("pe"))
        w = edge("ac") + rel("ar4")
        dv(lambda h: h.tensor_tensor(out=dlo, in0=klo[:, :], in1=qlo[:, :],
                                     op=ALU.subtract), waits=w)
        w = edge("dv")
        dv(lambda h: h.scalar_tensor_tensor(out=qt_sb[:, :], in0=dlo, scalar=ulow[:, :],
                                            in1=qlo[:, :], op0=ALU.mult, op1=ALU.add),
           waits=w + rel("qt_sb"))
        dv(lambda h: h.scalar_tensor_tensor(out=dlo, in0=dlo, scalar=negu[:, :],
                                            in1=klo[:, :], op0=ALU.mult, op1=ALU.add))
        w = edge("dv")
        REL["stg0"] = ("dv", P.cur("dv"))
        REL["stg1"] = ("dv", P.cur("dv"))
        sp_dma(kl.ap().rearrange("(hh d) t -> hh d t", hh=16)[:, 120:128, :],
               dlo, waits=w)
        REL["ar4"] = ("ld", P.cur("ld"))
        vthr = {}
        for vc in (0, 1):
            vthr[vc] = wunit("wh", vc % 2,
                             WH[vc % 2][:, 0:2048].rearrange("p (c f) -> p c f", c=16),
                             wa_r[:, :, 3840 + vc * 128: 3840 + (vc + 1) * 128])
        if EARLY_CC:
            collective(kl, kg, ccK, P.cur("ld"))
        for vc in range(16):
            si = vc % 2
            thr = vthr[vc] if vc in vthr else wunit(
                "wh", si, WH[si][:, 0:2048].rearrange("p (c f) -> p c f", c=16),
                wa_r[:, :, 3840 + vc * 128: 3840 + (vc + 1) * 128])
            vstg = STG[si]
            for th in range(2):
                proj_group(mm0[:, 0:512],
                           lambda c, si=si: WH[si][:, c * 128:(c + 1) * 128],
                           lambda c, th=th: lnfc(c)[:, th * 512:(th + 1) * 512],
                           thr, extra=lnf_rel + rel("mm0") +
                           (rel(f"stg{si}") if th == 0 else []))
                w2 = edge("pe")
                ac(lambda h, vstg=vstg, th=th: h.activation(
                    out=vstg[:, th * 512:(th + 1) * 512], in_=mm0[:, 0:512], func=AF.Copy,
                    bias=0.0, scale=1.0), waits=w2)
                REL["mm0"] = ("ac", P.cur("ac"))
            REL[f"wh{si}"] = ("pe", P.cur("pe"))
            w2 = edge("ac")
            for jb in range(2):
                tiles = [vstg[:, (jb * 4 + t) * 128:(jb * 4 + t + 1) * 128] for t in range(4)]
                dsts = [AR[3][:, (jb * 4 + t) * 128:(jb * 4 + t + 1) * 128] for t in range(4)]
                transpose_batch(tiles, dsts, (w2 + rel("ar3")) if jb == 0 else [])
            REL[f"stg{si}"] = ("pe", P.cur("pe"))
            w2 = edge("dv")
            sp_dma(vl.ap().rearrange("(jt p) f -> p jt f", jt=8)[:, :, vc * 128:(vc + 1) * 128],
                   AR[3].ap()[:, 0:1024].rearrange("p (jt f) -> p jt f", jt=8), waits=w2)
            REL["ar3"] = ("ld", P.cur("ld"))
        dbg("qt_sb", qt_sb[:, :], (128, 1024), BF16, [("dv", P.cur("dv"))])
        qpre = {}
        for hh in (0, 1):
            qpre[hh] = wunit("wh", hh % 2,
                             WH[hh % 2][:, 0:1920].rearrange("p (c f) -> p c f", c=16),
                             wa_r[:, :, hh * DS:(hh + 1) * DS])
        kvw = P.cur("ld")
        dbg("k1l", kl.ap()[:, :], tuple(kl.shape), BF16, [("ld", kvw)])
        dbg("v1l", vl.ap()[:, :], tuple(vl.shape), BF16, [("ld", kvw)])
        if not EARLY_CC:
            collective(kl, kg, ccK, kvw)
        collective(vl, vg, ccV, kvw)

        def ffn_stageA1(j, qs):
            jp = j % 2
            Pun = AR[4 + jp]
            bt = btileP[:, jp:jp + 1]
            qb = qs["qb"]
            sqkh = qs["sqk"]
            dv(lambda h, j=j, bt=bt, qb=qb, sqkh=sqkh: h.tensor_scalar(
                out=bt, in0=qns[:, qb + j:qb + j + 1],
                scalar1=sqkh[:, :], scalar2=-RSQD,
                op0=ALU.mult, op1=ALU.mult),
               waits=list(qs["sqk_dv"]) + list(qs["qns_ac"]) + rel(f"bt{jp}"))
            bt_dv = edge("dv")
            lhsT_ap = qs["Qf"][:, j * 128:(j + 1) * 128]
            nch = score_chunks(j, lhsT_ap, qs["Kg"], Pun, bt, bt_dv, 0,
                               list(qs["kld"]) + list(qs["qf_ld"]), [f"ar{4 + jp}"])
            REL[f"bt{jp}"] = ("ac", P.cur("ac"))
            return {"exp": edge("ac"), "Pun": Pun, "jp": jp, "nch": nch}

        def ffn_stageA2(j, st):
            jp = st["jp"]
            tok_exp = st["exp"]
            nch = st["nch"]
            dv(lambda h, nch=nch: h.reduce_sum(out=ssum[:, :], in_=scol[:, 0:nch],
                                               axis=AX.X), waits=tok_exp + rel("ssum"))
            REL["scol"] = ("dv", P.cur("dv"))
            w2 = edge("dv")
            dv(lambda h, jp=jp: h.reciprocal(out=rsP[:, jp:jp + 1], in_=ssum[:, :]), waits=w2)
            REL["ssum"] = ("dv", P.cur("dv"))
            st["rs"] = edge("dv")
            return st

        def ffn_stageB(j, st, Vg_s, h_, vld):
            jp = st["jp"]
            dco = attf[:, h_ * 1024 + j * 128: h_ * 1024 + (j + 1) * 128]

            def scal_fn(w_av, avt_t, jp=jp, st=st):
                return rsP[:, jp:jp + 1], st["rs"]

            relP = attn_tailB(j, st["Pun"], Vg_s, dco, st["exp"], scal_fn, None, vwait=vld)
            REL[f"ar{4 + jp}"] = relP

        def q_parts(h2):
            si = h2 % 2
            qb = 8 * si
            sqk_dst = sqk if si == 0 else sqkb
            kb = 64 if si == 0 else 80
            qst = {"Qf": STG[si], "Kg": AR[0 + 2 * si], "Vg": AR[1 + 2 * si],
                   "qb": qb, "sqk": sqk_dst}
            scl = scstd[0:120, h2:h2 + 1]

            def p_w():
                if h2 in qpre:
                    qst["thr"] = qpre[h2]
                else:
                    qst["thr"] = wunit("wh", si,
                                       WH[si][:, 0:1920].rearrange("p (c f) -> p c f", c=16),
                                       wa_r[:, :, h2 * DS:(h2 + 1) * DS])

            def p_proj():
                Qf = qst["Qf"]
                for th in range(2):
                    proj_group(mm0[0:120, 0:512],
                               lambda c, si=si: WH[si][:, c * DS:(c + 1) * DS],
                               lambda c, th=th: lnfc(c)[:, th * 512:(th + 1) * 512],
                               qst["thr"], extra=lnf_rel + rel("mm0") +
                               (rel(f"stg{si}") if th == 0 else []))
                    w2 = edge("pe")
                    ac(lambda h, Qf=Qf, th=th, scl=scl: h.activation(
                        out=Qf[0:120, th * 512:(th + 1) * 512], in_=mm0[0:120, 0:512],
                        func=AF.Copy, bias=0.0, scale=scl), waits=w2)
                    REL["mm0"] = ("ac", P.cur("ac"))
                REL[f"wh{si}"] = ("pe", P.cur("pe"))
                w2 = edge("ac")
                sp_dma(Qf[120:128, :], qt_sb[h2 * 8:(h2 + 1) * 8, :], waits=w2)
                qst["qf_ld"] = [("ld", P.cur("ld"))]

            def p_gk():
                gather_k(qst["Kg"], kg.ap(), h2 * 128, [(ccK, 1)] + rel(f"ar{0 + 2 * si}"))
                qst["kld"] = [("ld", P.cur("ld"))]

            def p_gv():
                gather_v(qst["Vg"], vg.ap(), h2 * 128, [(ccV, 1)] + rel(f"ar{1 + 2 * si}"))
                qst["vld"] = [("ld", P.cur("ld"))]

            def p_qn():
                qnorm_chain(qst["Qf"], 1024, list(range(qb, qb + 8)), qst["qf_ld"])
                qst["qns_ac"] = edge("ac")

            def p_kn():
                knorm_chain(qst["Kg"], qst["kld"], sqk_dst, kb)
                qst["sqk_dv"] = edge("dv")

            return qst, p_w, p_proj, p_gk, p_gv, p_qn, p_kn

        st_q = {}
        pa = q_parts(0)
        pb = q_parts(1)
        pa[1](); pa[2]()
        pb[1](); pb[2]()
        pa[3](); pa[5](); pa[6]()
        pb[3](); pb[5](); pb[6]()
        pa[4](); pb[4]()
        st_q[0] = pa[0]
        st_q[1] = pb[0]

        for h_ in range(16):
            si = h_ % 2
            qs = st_q.pop(h_)
            Vg_s = qs["Vg"]
            vld = qs["vld"]
            nxt = q_parts(h_ + 2) if h_ + 2 < 16 else None
            stA = {}
            for j in range(8):
                a1 = ffn_stageA1(j, qs)
                if j == 0 and nxt is not None:
                    nxt[1]()
                if j > 0:
                    ffn_stageB(j - 1, stA[j - 1], Vg_s, h_, vld)
                stA[j] = ffn_stageA2(j, a1)
            REL[f"stg{si}"] = ("pe", P.cur("pe"))
            REL[f"ar{0 + 2 * si}"] = ("pe", P.cur("pe"))
            if nxt is not None:
                nxt[2]()
                nxt[3]()
            ffn_stageB(7, stA[7], Vg_s, h_, vld)
            REL[f"ar{1 + 2 * si}"] = ("pe", P.cur("pe"))
            if nxt is not None:
                nxt[4]()
                nxt[5]()
                nxt[6]()
                st_q[h_ + 2] = nxt[0]
        REL["qt_sb"] = ("ld", P.cur("ld"))
        attf_dv = [("dv", P.cur("dv"))]
        dbg("attf", attf[:, :], (128, 16384), BF16, attf_dv)

        resid_tok = [None] * 8
        for cb in range(4):
            thr0 = wunit("wb", 0, WB[0][:, :].rearrange("p (c f) -> p c f", c=8),
                         wp_r[:, 0:8, cb * 512:(cb + 1) * 512])
            thr1 = wunit("wb", 1, WB[1][:, :].rearrange("p (c f) -> p c f", c=8),
                         wp_r[:, 8:16, cb * 512:(cb + 1) * 512])
            for j in range(8):
                for h_ in range(16):
                    si = h_ // 8
                    lhs = attf[:, h_ * 1024 + j * 128: h_ * 1024 + (j + 1) * 128]
                    rhsap = WB[si][:, (h_ % 8) * 512:((h_ % 8) + 1) * 512]
                    wt = ()
                    if h_ == 0:
                        wt = [("wd", thr0)] + attf_dv + rel("mm0")
                    elif h_ == 8:
                        wt = [("wd", thr1)]
                    pe(lambda h, lhs=lhs, rhsap=rhsap, h_=h_: h.matmul(
                        out=mm0[:, 0:512], lhsT=lhs, rhs=rhsap,
                        start=(h_ == 0), stop=(h_ == 15)), waits=wt,
                        inc=1 if h_ == 15 else 0)
                w2 = edge("pe")
                xt = xs[j][:, cb * 512:(cb + 1) * 512]
                dv(lambda h, xt=xt: h.scalar_tensor_tensor(
                    out=xt, in0=mm0[:, 0:512], scalar=0.5, in1=xt,
                    op0=ALU.mult, op1=ALU.add), waits=w2)
                REL["mm0"] = ("dv", P.cur("dv"))
                if cb == 3:
                    resid_tok[j] = ("dv", P.cur("dv"))
            REL["wb0"] = ("pe", P.cur("pe"))
            REL["wb1"] = ("pe", P.cur("pe"))
        REL["attf_w"] = ("pe", P.cur("pe"))
        REL["lnf_w"] = ("pe", P.cur("pe"))
        return resid_tok

    def build_attn(xdep):
        kl, kg, vl, vg = colld[1]
        ccK, ccV = "cc2", "cc3"
        build_ln(xdep)
        lnf_rel = [REL["lnf"]]

        def rope(pre, a, w_pre, dre=None):
            sp_dma(rr["im"], pre[64:128, :],
                   waits=list(w_pre) + [("dv", P.cur("dv"))] + rel("ar5"))
            im_ld = [("ld", P.cur("ld"))]
            ca, sa = cosT[a], sinT[a]
            dv(lambda h, pre=pre, ca=ca: h.tensor_tensor(
                out=rr["t1"], in0=pre[0:64, :], in1=ca[:, :], op=ALU.mult),
                waits=list(w_pre) + rel("rope") + rel("ar5") + rel("ar6"))
            dv(lambda h, sa=sa: h.tensor_tensor(
                out=rr["t2"], in0=rr["im"], in1=sa[:, :], op=ALU.mult),
                waits=im_ld)
            w2 = edge("dv")
            ore_dst = dre if dre is not None else rr["t1"]
            dv(lambda h, ore_dst=ore_dst: h.tensor_tensor(
                out=ore_dst, in0=rr["t1"], in1=rr["t2"], op=ALU.subtract),
                waits=w2)
            dv(lambda h, pre=pre, sa=sa: h.tensor_tensor(
                out=rr["t2"], in0=pre[0:64, :], in1=sa[:, :], op=ALU.mult))
            dv(lambda h, ca=ca: h.tensor_tensor(
                out=rr["im"], in0=rr["im"], in1=ca[:, :], op=ALU.mult))
            w2 = edge("dv")
            dv(lambda h: h.tensor_tensor(out=rr["t2"], in0=rr["t2"],
                                         in1=rr["im"], op=ALU.add), waits=w2)
            return edge("dv")

        for kvh in range(4):
            for a in range(2):
                si = (kvh * 2 + a) % 2
                thr = wunit("wh", si, WH[si][:, 0:2048].rearrange("p (c f) -> p c f", c=16),
                            wk_r[:, :, (kvh * 2 + a) * 128:(kvh * 2 + a + 1) * 128])
                pre = STG[si]
                for th in range(2):
                    proj_group(mm0[:, 0:512],
                               lambda c, si=si: WH[si][:, c * 128:(c + 1) * 128],
                               lambda c, th=th: lnfc(c)[:, th * 512:(th + 1) * 512],
                               thr, extra=lnf_rel + rel("mm0") +
                               (rel(f"stg{si}") if th == 0 else []))
                    w2 = edge("pe")
                    ac(lambda h, pre=pre, th=th: h.activation(
                        out=pre[:, th * 512:(th + 1) * 512], in_=mm0[:, 0:512],
                        func=AF.Copy, bias=0.0, scale=1.0), waits=w2)
                    REL["mm0"] = ("ac", P.cur("ac"))
                REL[f"wh{si}"] = ("pe", P.cur("pe"))
                w_pre = edge("ac")
                wr = rope(pre, a, w_pre)
                REL[f"stg{si}"] = ("dv", P.cur("dv"))
                row0 = (kvh * 2 + a) * 128
                sp_dma(kl.ap()[row0:row0 + 64, :], rr["t1"], waits=wr)
                sp_dma(kl.ap()[row0 + 64:row0 + 128, :], rr["t2"], waits=wr)
                REL["rope"] = ("ld", P.cur("ld"))
                REL["ar5"] = ("ld", P.cur("ld"))
                REL["ar6"] = ("ld", P.cur("ld"))
        vthr = {}
        for vc in (0, 1):
            vthr[vc] = wunit("wh", vc % 2,
                             WH[vc % 2][:, 0:2048].rearrange("p (c f) -> p c f", c=16),
                             wv_r[:, :, vc * 128:(vc + 1) * 128])
        if EARLY_CC:
            collective(kl, kg, ccK, P.cur("ld"))
        for vc in range(4):
            si = vc % 2
            thr = vthr[vc] if vc in vthr else wunit(
                "wh", si, WH[si][:, 0:2048].rearrange("p (c f) -> p c f", c=16),
                wv_r[:, :, vc * 128:(vc + 1) * 128])
            vstg = STG[si]
            for th in range(2):
                proj_group(mm0[:, 0:512],
                           lambda c, si=si: WH[si][:, c * 128:(c + 1) * 128],
                           lambda c, th=th: lnfc(c)[:, th * 512:(th + 1) * 512],
                           thr, extra=lnf_rel + rel("mm0") +
                           (rel(f"stg{si}") if th == 0 else []))
                w2 = edge("pe")
                ac(lambda h, vstg=vstg, th=th: h.activation(
                    out=vstg[:, th * 512:(th + 1) * 512], in_=mm0[:, 0:512], func=AF.Copy,
                    bias=0.0, scale=1.0), waits=w2)
                REL["mm0"] = ("ac", P.cur("ac"))
            REL[f"wh{si}"] = ("pe", P.cur("pe"))
            w2 = edge("ac")
            for jb in range(2):
                tiles = [vstg[:, (jb * 4 + t) * 128:(jb * 4 + t + 1) * 128] for t in range(4)]
                dsts = [AR[3][:, (jb * 4 + t) * 128:(jb * 4 + t + 1) * 128] for t in range(4)]
                transpose_batch(tiles, dsts, (w2 + rel("ar3")) if jb == 0 else [])
            REL[f"stg{si}"] = ("pe", P.cur("pe"))
            w2 = edge("dv")
            sp_dma(vl.ap().rearrange("(jt p) f -> p jt f", jt=8)[:, :, vc * 128:(vc + 1) * 128],
                   AR[3].ap()[:, 0:1024].rearrange("p (jt f) -> p jt f", jt=8), waits=w2)
            REL["ar3"] = ("ld", P.cur("ld"))
        qpre = {}
        for aa in (0, 1):
            qpre[(0, aa)] = wunit("wh", aa % 2,
                                  WH[aa % 2][:, 0:2048].rearrange("p (c f) -> p c f", c=16),
                                  wq_r[:, :, aa * 128:(aa + 1) * 128])
        if not EARLY_CC:
            collective(kl, kg, ccK, P.cur("ld"))
        collective(vl, vg, ccV, P.cur("ld"))

        PEX = [WB[0][:, 0:2048], WB[1][:, 0:2048]]

        def attn_stageA1(j, Qr, Kg0, Kg1, kv_ld, q_ready, qns_ac, sqk_dv, h_):
            jp = j % 2
            bt = btileP[:, jp:jp + 1]
            bt2 = btile2P[:, jp:jp + 1]
            dv(lambda h, j=j, bt=bt: h.tensor_scalar(out=bt, in0=qns[:, j:j + 1],
                                                     scalar1=sqk[:, :], scalar2=-RSQD,
                                                     op0=ALU.mult, op1=ALU.mult),
               waits=list(sqk_dv) + list(qns_ac) + rel(f"bt{jp}"))
            dv(lambda h, j=j, bt2=bt2: h.tensor_scalar(out=bt2, in0=qns[:, 8 + j:9 + j],
                                                       scalar1=sqkb[:, :], scalar2=-RSQD,
                                                       op0=ALU.mult, op1=ALU.mult))
            bt_dv = edge("dv")
            P0, P1 = AR[4], AR[5]
            score_chunks(j, Qr[:, j * 128:(j + 1) * 128], Kg0, P0, bt, bt_dv, 0,
                         list(kv_ld) + list(q_ready), ["ar4"])
            score_chunks(j, Qr[:, 1024 + j * 128: 1024 + (j + 1) * 128], Kg1, P1,
                         bt2, bt_dv, 4, list(kv_ld) + list(q_ready), ["ar5"])
            REL[f"bt{jp}"] = ("ac", P.cur("ac"))
            return {"exp1": edge("ac"), "jp": jp}

        def attn_stageA2(j, st, h_):
            jp = st["jp"]
            Lp = LPAD[j] * 128
            P0, P1 = AR[4], AR[5]
            tok_exp = st["exp1"]
            nch = (Lp + 511) // 512
            dv(lambda h, nch=nch: h.reduce_sum(out=ssum[:, :], in_=scol[:, 0:nch],
                                               axis=AX.X), waits=tok_exp + rel("ssum"))
            dv(lambda h, nch=nch: h.reduce_sum(out=ssum2[:, :], in_=scol[:, 4:4 + nch],
                                               axis=AX.X))
            REL["scol"] = ("dv", P.cur("dv"))
            w2 = edge("dv")
            if FUSE_COMB:
                dv(lambda h, jp=jp: h.reciprocal(out=rsA[:, jp:jp + 1], in_=ssum[:, :]),
                   waits=w2 + rel(f"rsA{jp}"))
                dv(lambda h: h.reciprocal(out=rs2[:, :], in_=ssum2[:, :]))
                w2 = edge("dv")
                nlh = neglam[:, h_:h_ + 1]
                dv(lambda h, nlh=nlh: h.tensor_scalar(out=gamc[:, :], in0=ssum[:, :],
                                                      scalar1=rs2[:, :], scalar2=nlh,
                                                      op0=ALU.mult, op1=ALU.mult), waits=w2)
                w2 = edge("dv")
                dv(lambda h, Lp=Lp: h.scalar_tensor_tensor(
                    out=P0[:, 0:Lp], in0=P1[:, 0:Lp], scalar=gamc[:, :], in1=P0[:, 0:Lp],
                    op0=ALU.mult, op1=ALU.add), waits=w2)
                REL["ar5"] = ("dv", P.cur("dv"))
                expsc = rsA[:, jp:jp + 1]
            else:
                dv(lambda h: h.reciprocal(out=rs[:, :], in_=ssum[:, :]), waits=w2)
                dv(lambda h: h.reciprocal(out=rs2[:, :], in_=ssum2[:, :]))
                w2 = edge("dv")
                lamh = lam_b[:, h_:h_ + 1]
                dv(lambda h, lamh=lamh: h.tensor_scalar(out=beta[:, :], in0=rs2[:, :],
                                                        scalar1=lamh, scalar2=-1.0,
                                                        op0=ALU.mult, op1=ALU.mult), waits=w2)
                w2 = edge("dv")
                dv(lambda h, Lp=Lp: h.tensor_scalar(out=P0[:, 0:Lp], in0=P0[:, 0:Lp],
                                                    scalar1=rs[:, :], scalar2=None,
                                                    op0=ALU.mult), waits=w2)
                dv(lambda h, Lp=Lp: h.tensor_scalar(out=P1[:, 0:Lp], in0=P1[:, 0:Lp],
                                                    scalar1=beta[:, :], scalar2=None,
                                                    op0=ALU.mult))
                w2 = edge("dv")
                dv(lambda h, Lp=Lp: h.tensor_tensor(out=P0[:, 0:Lp], in0=P0[:, 0:Lp],
                                                    in1=P1[:, 0:Lp], op=ALU.add), waits=w2)
                REL["ar5"] = ("dv", P.cur("dv"))
                expsc = 1.0
            w2 = edge("dv")
            dv(lambda h, Lp=Lp, j=j: h.tensor_tensor(
                out=P0[:, Lp - 256:Lp], in0=P0[:, Lp - 256:Lp],
                in1=masksb[:, j * 256:(j + 1) * 256], op=ALU.add), waits=w2)
            w2 = edge("dv")
            ac(lambda h, Lp=Lp, jp=jp, expsc=expsc: h.activation(
                out=PEX[jp][:, 0:Lp], in_=P0[:, 0:Lp],
                func=AF.Exp, bias=negone[:, :], scale=expsc,
                accum_out=s2P[:, jp:jp + 1]),
               waits=w2 + rel(f"wb{jp}"))
            if FUSE_COMB:
                REL[f"rsA{jp}"] = ("ac", P.cur("ac"))
            REL["ar4"] = ("ac", P.cur("ac"))
            REL["ssum"] = ("dv", P.cur("dv"))
            return {"exp": edge("ac"), "jp": jp}

        def attn_stageB(j, st, Vg_s, h_, vld):
            jp = st["jp"]
            dco = attf[:, h_ * 1024 + j * 128: h_ * 1024 + (j + 1) * 128]

            def scal_fn(w_av, avt_t, jp=jp, st=st):
                ac(lambda h, avt_t=avt_t: h.activation(out=sqsm[:, 0:128], in_=avt_t,
                                                       func=AF.Square,
                                                       accum_out=m2c[:, :]),
                   waits=list(w_av) + rel("sqsm") + list(st["exp"]) + edge("dv"))
                REL["sqsm"] = ("ac", P.cur("ac"))
                w2 = edge("ac")
                dv(lambda h, jp=jp: h.tensor_scalar(out=s2sq[:, :], in0=s2P[:, jp:jp + 1],
                                                    scalar1=s2P[:, jp:jp + 1], scalar2=1e-8,
                                                    op0=ALU.mult, op1=ALU.mult), waits=w2)
                w2 = edge("dv")
                dv(lambda h: h.scalar_tensor_tensor(out=den[:, :], in0=m2c[:, :],
                                                    scalar=1.0 / 128.0, in1=s2sq[:, :],
                                                    op0=ALU.mult, op1=ALU.add), waits=w2)
                w2 = edge("dv")
                ac(lambda h: h.activation(out=rstdc[:, :], in_=den[:, :], func=AF.Ln,
                                          bias=zcol[:, :], scale=1.0), waits=w2 + edge("ac"))
                w2 = edge("ac")
                ac(lambda h: h.activation(out=rstdc[:, :], in_=rstdc[:, :], func=AF.Exp,
                                          bias=zcol[:, :], scale=-0.5), waits=w2)
                return rstdc[:, :], edge("ac")

            relP = attn_tailB(j, PEX[jp], Vg_s, dco, st["exp"], scal_fn, rms08[:, :],
                              vwait=vld)
            REL[f"wb{jp}"] = relP

        kv_ld = []
        sqk_dv = []
        for h_ in range(16):
            kvh = h_ // 4
            Qr, Kg0, Kg1, Vg_s = AR[3], AR[0], AR[1], AR[2]
            for a in range(2):
                sia = (2 * h_ + a) % 2
                key = (h_, a)
                if key in qpre:
                    thr = qpre[key]
                else:
                    thr = wunit("wh", sia,
                                WH[sia][:, 0:2048].rearrange("p (c f) -> p c f", c=16),
                                wq_r[:, :, h_ * 256 + a * 128: h_ * 256 + (a + 1) * 128])
                pre = STG[sia]
                si = sia
                for th in range(2):
                    proj_group(mm0[:, 0:512],
                               lambda c, si=si: WH[si][:, c * 128:(c + 1) * 128],
                               lambda c, th=th: lnfc(c)[:, th * 512:(th + 1) * 512],
                               thr, extra=lnf_rel + rel("mm0") +
                               (rel(f"stg{si}") if th == 0 else []))
                    w2 = edge("pe")
                    ac(lambda h, pre=pre, th=th: h.activation(
                        out=pre[:, th * 512:(th + 1) * 512], in_=mm0[:, 0:512],
                        func=AF.Copy, bias=0.0, scale=1.0), waits=w2)
                    REL["mm0"] = ("ac", P.cur("ac"))
                w_pre = edge("ac") + (rel("ar3") if a == 0 else [])
                wr = rope(pre, a, w_pre, dre=Qr[0:64, a * 1024:(a + 1) * 1024])
                REL[f"stg{si}"] = ("dv", P.cur("dv"))
                sp_dma(Qr[64:128, a * 1024:(a + 1) * 1024], rr["t2"], waits=wr)
                REL["rope"] = ("ld", P.cur("ld"))
                REL["ar5"] = ("ld", P.cur("ld"))
                REL["ar6"] = ("ld", P.cur("ld"))
                REL[f"wh{si}"] = ("pe", P.cur("pe"))
            q_ready = [("ld", P.cur("ld"))] + edge("dv")
            if h_ % 4 == 0:
                gather_k(Kg0, kg.ap(), (kvh * 2 + 0) * 128, [(ccK, 1)] + rel("ar0"))
                gather_k(Kg1, kg.ap(), (kvh * 2 + 1) * 128, [(ccK, 1)] + rel("ar1"))
                kv_ld0 = [("ld", P.cur("ld"))]
                gather_v(Vg_s, vg.ap(), kvh * 128, [(ccV, 1)] + rel("ar2"))
                kv_ld = [("ld", P.cur("ld"))]
                knorm_chain(Kg0, kv_ld0, sqk, 64)
                knorm_chain(Kg1, kv_ld0, sqkb, 80)
                sqk_dv = edge("dv")
            qnorm_chain(Qr[:, 0:1024], 1024, list(range(8)), q_ready)
            qnorm_chain(Qr[:, 1024:2048], 1024, list(range(8, 16)), q_ready)
            qns_ac = edge("ac")
            stA = {}
            for j in range(8):
                a1 = attn_stageA1(j, Qr, Kg0, Kg1, kv_ld0, q_ready, qns_ac, sqk_dv, h_)
                if j > 0:
                    attn_stageB(j - 1, stA[j - 1], Vg_s, h_, kv_ld)
                stA[j] = attn_stageA2(j, a1, h_)
            attn_stageB(7, stA[7], Vg_s, h_, kv_ld)
            REL["ar3"] = ("pe", P.cur("pe"))
            if h_ % 4 == 3:
                REL["ar0"] = ("pe", P.cur("pe"))
                REL["ar1"] = ("pe", P.cur("pe"))
                REL["ar2"] = ("pe", P.cur("pe"))
            REL[f"stg{si}"] = ("pe", P.cur("pe"))
        attf_dv = [("dv", P.cur("dv"))]

        resid_tok = [None] * 8
        for cb in range(4):
            thr0 = wunit("wb", 0, WB[0][:, :].rearrange("p (c f) -> p c f", c=8),
                         wo_r[:, 0:8, cb * 512:(cb + 1) * 512])
            thr1 = wunit("wb", 1, WB[1][:, :].rearrange("p (c f) -> p c f", c=8),
                         wo_r[:, 8:16, cb * 512:(cb + 1) * 512])
            for j in range(8):
                for h_ in range(16):
                    si = h_ // 8
                    lhs = attf[:, h_ * 1024 + j * 128: h_ * 1024 + (j + 1) * 128]
                    rhsap = WB[si][:, (h_ % 8) * 512:((h_ % 8) + 1) * 512]
                    wt = ()
                    if h_ == 0:
                        wt = [("wd", thr0)] + attf_dv + rel("mm0")
                    elif h_ == 8:
                        wt = [("wd", thr1)]
                    pe(lambda h, lhs=lhs, rhsap=rhsap, h_=h_: h.matmul(
                        out=mm0[:, 0:512], lhsT=lhs, rhs=rhsap,
                        start=(h_ == 0), stop=(h_ == 15)), waits=wt,
                        inc=1 if h_ == 15 else 0)
                w2 = edge("pe")
                xt = xs[j][:, cb * 512:(cb + 1) * 512]
                dv(lambda h, xt=xt: h.tensor_tensor(out=xt, in0=mm0[:, 0:512], in1=xt,
                                                    op=ALU.add), waits=w2)
                REL["mm0"] = ("dv", P.cur("dv"))
                if cb == 3:
                    resid_tok[j] = ("dv", P.cur("dv"))
            REL["wb0"] = ("pe", P.cur("pe"))
            REL["wb1"] = ("pe", P.cur("pe"))
        REL["attf_w"] = ("pe", P.cur("pe"))
        REL["lnf_w"] = ("pe", P.cur("pe"))
        return resid_tok

    dep = x_ld
    if N_SUB >= 1:
        dep = build_ffn(0, dep)
    if N_SUB >= 2:
        dep = build_attn(dep)
    if N_SUB >= 3:
        dep = build_ffn(2, dep)

    for j in range(8):
        P.emit("sp", lambda h, j=j: h.dma_start(out=y_out[j * 128:(j + 1) * 128, :],
                                                in_=xs[j][:, :]),
               waits=[dep[j]], inc=[("st", 16)])
    P.emit("sp", None, waits=[("st", P.cur("st"))])

    with nc.Block() as blk:
        @blk.tensor
        def _(t):
            for f in P.ops["pe"]:
                f(t)

        @blk.vector
        def _(v):
            for f in P.ops["dv"]:
                f(v)

        @blk.scalar
        def _(s):
            for f in P.ops["ac"]:
                f(s)

        @blk.gpsimd
        def _(g):
            for f in P.ops["gp"]:
                f(g)

        @blk.sync
        def _(s):
            for f in P.ops["sp"]:
                f(s)

    return nc


_NC_CACHE = []


def _rope_perm():
    idx = []
    for h_ in range(16):
        for a in range(2):
            base = h_ * 256 + a * 128
            idx += [base + 2 * j for j in range(64)]
            idx += [base + 2 * j + 1 for j in range(64)]
    return np.array(idx)


def _rope_kperm():
    idx = []
    for kvh in range(4):
        for a in range(2):
            base = kvh * 256 + a * 128
            idx += [base + 2 * j for j in range(64)]
            idx += [base + 2 * j + 1 for j in range(64)]
    return np.array(idx)


def _host_tables(srank):
    blocks = SET[srank]
    pos = np.concatenate([np.arange(b * 128, (b + 1) * 128) for b in blocks]).astype(np.float32)
    theta = (1.0 / (10000.0 ** (np.arange(0, 256, 2, dtype=np.float32) / 256.0)))
    ang = pos[None, :] * theta[:, None]
    cos = np.cos(ang).astype(np.float32)
    sin = np.sin(ang).astype(np.float32)
    bf = ml_dtypes.bfloat16
    cs = [cos[0:64].astype(bf), sin[0:64].astype(bf),
          cos[64:128].astype(bf), sin[64:128].astype(bf)]
    m = np.zeros((8, 128, 256), dtype=np.float32)
    for j in range(8):
        i = blocks[j]
        Lp = LPAD[j] * 128
        kglob = Lp - 256 + np.arange(256)
        qglob = i * 128 + np.arange(128)
        m[j] = np.where(kglob[None, :] > qglob[:, None], NEG, 0.0)
    maskS = m.reshape(1024, 256).astype(bf)
    return cs, maskS


def kernel(L0_attn_x, L0_attn_Wq, L0_attn_Wk, L0_attn_Wv, L0_attn_Wo,
           L0_attn_lq1, L0_attn_lq2, L0_attn_lk1, L0_attn_lk2, L0_attn_rms_w,
           L0_ffn_Wa, L0_ffn_Wp, L0_ffn_w_std, L0_ffn_w_rec):
    x = np.ascontiguousarray(np.asarray(L0_attn_x, dtype=np.float32))
    wq = np.ascontiguousarray(np.asarray(L0_attn_Wq, dtype=np.float32)[:, _rope_perm()])
    wk = np.ascontiguousarray(np.asarray(L0_attn_Wk, dtype=np.float32)[:, _rope_kperm()])
    wv = np.ascontiguousarray(np.asarray(L0_attn_Wv, dtype=np.float32))
    wo = np.ascontiguousarray(np.asarray(L0_attn_Wo, dtype=np.float32))
    wa = np.ascontiguousarray(np.asarray(L0_ffn_Wa, dtype=np.float32))
    wp = np.ascontiguousarray(np.asarray(L0_ffn_Wp, dtype=np.float32))
    wstd = np.asarray(L0_ffn_w_std, dtype=np.float32).reshape(16, 1)
    wrec = np.asarray(L0_ffn_w_rec, dtype=np.float32).reshape(16, 1)
    rmsw = np.asarray(L0_attn_rms_w, dtype=np.float32).reshape(128, 1)
    lq1 = np.asarray(L0_attn_lq1, dtype=np.float32)
    lq2 = np.asarray(L0_attn_lq2, dtype=np.float32)
    lk1 = np.asarray(L0_attn_lk1, dtype=np.float32)
    lk2 = np.asarray(L0_attn_lk2, dtype=np.float32)

    bf = ml_dtypes.bfloat16
    ident = np.eye(128, dtype=bf)
    sel16 = np.zeros((16, 128), dtype=np.float32)
    for hh in range(16):
        sel16[hh, hh * 8:(hh + 1) * 8] = 1.0
    sel16 = sel16.astype(bf)

    if not _NC_CACHE:
        _NC_CACHE.append(_build_program())
    nc = _NC_CACHE[0]

    in_maps = []
    for c in range(8):
        b, s_ = c // 2, c % 2
        blocks = SET[s_]
        xl = np.concatenate([x[b, bl * 128:(bl + 1) * 128, :] for bl in blocks], axis=0)
        cs, maskS = _host_tables(s_)
        in_maps.append({
            "x": np.ascontiguousarray(xl),
            "wa": wa, "wp": wp, "wq": wq, "wk": wk, "wv": wv, "wo": wo,
            "wstd": wstd, "wrec": wrec, "rmsw": rmsw,
            "lq1": lq1, "lq2": lq2, "lk1": lk1, "lk2": lk2,
            "ident": ident, "maskS": maskS, "sel16": sel16,
            "cos0": cs[0], "sin0": cs[1], "cos1": cs[2], "sin1": cs[3],
        })
    res = run_bass_kernel_spmd(nc, in_maps, core_ids=list(range(8)))
    out = np.empty((4, 2048, 2048), dtype=np.float32)
    for c in range(8):
        b, s_ = c // 2, c % 2
        y = np.asarray(res.results[c]["y"])
        for j, bl in enumerate(SET[s_]):
            out[b, bl * 128:(bl + 1) * 128, :] = y[j * 128:(j + 1) * 128, :]
    return out


# version 12
# speedup vs baseline: 1.0765x; 1.0765x over previous
import math

import numpy as np
import ml_dtypes

import concourse.bass as bass
import concourse.mybir as mybir
from concourse.bass_utils import run_bass_kernel_spmd

F32 = mybir.dt.float32
BF16 = mybir.dt.bfloat16
AF = mybir.ActivationFunctionType
ALU = mybir.AluOpType
AX = mybir.AxisListType

CC = 2048
SET = [[0, 3, 4, 7, 8, 11, 12, 15], [1, 2, 5, 6, 9, 10, 13, 14]]
LPAD = [2, 4, 6, 8, 10, 12, 14, 16]
RSQD = 1.0 / math.sqrt(128.0)
NEG = -1.0e30
DS = 120
GROUPS = [[0, 1], [2, 3], [4, 5], [6, 7]]
ENGS = ("pe", "dv", "ac", "gp", "sp")

N_SUB = 3
DEV_DBG = False
FUSE_COMB = True
EARLY_CC = False


class Plan:
    def __init__(self, nc):
        self.nc = nc
        self.ops = {e: [] for e in ENGS}
        self.sems = {}
        self.counts = {}
        self._lastwait = {}

    def sem(self, name):
        self.sems[name] = self.nc.alloc_semaphore(name)
        self.counts[name] = 0

    def cur(self, name):
        return self.counts[name]

    def emit(self, eng, fn=None, waits=(), inc=()):
        wl = []
        for n, v in waits:
            if v <= 0:
                continue
            k = (eng, n)
            if self._lastwait.get(k, -1) >= v:
                continue
            self._lastwait[k] = v
            wl.append((self.sems[n], v))
        il = []
        for n, k in inc:
            self.counts[n] += k
            il.append((self.sems[n], k))

        def run(h, wl=wl, fn=fn, il=il):
            for s, v in wl:
                h.wait_ge(s, v)
            if fn is not None:
                inst = fn(h)
                for s, k in il:
                    inst.then_inc(s, k)

        self.ops[eng].append(run)


def _build_program(DEV_DBG=None):
    if DEV_DBG is None:
        DEV_DBG = globals()["DEV_DBG"]
    nc = bass.Bass("TRN2", target_bir_lowering=False)
    dt = nc.dram_tensor

    x_in = dt("x", [1024, CC], F32, kind="ExternalInput").ap()
    y_out = dt("y", [1024, CC], F32, kind="ExternalOutput").ap()
    wa_d = dt("wa", [CC, 6144], F32, kind="ExternalInput").ap()
    wp_d = dt("wp", [CC, CC], F32, kind="ExternalInput").ap()
    wq_d = dt("wq", [CC, 4096], F32, kind="ExternalInput").ap()
    wk_d = dt("wk", [CC, 1024], F32, kind="ExternalInput").ap()
    wv_d = dt("wv", [CC, 512], F32, kind="ExternalInput").ap()
    wo_d = dt("wo", [CC, CC], F32, kind="ExternalInput").ap()
    wstd_d = dt("wstd", [16, 1], F32, kind="ExternalInput").ap()
    wrec_d = dt("wrec", [16, 1], F32, kind="ExternalInput").ap()
    rmsw_d = dt("rmsw", [128, 1], F32, kind="ExternalInput").ap()
    lq1_d = dt("lq1", [16, 128], F32, kind="ExternalInput").ap()
    lq2_d = dt("lq2", [16, 128], F32, kind="ExternalInput").ap()
    lk1_d = dt("lk1", [16, 128], F32, kind="ExternalInput").ap()
    lk2_d = dt("lk2", [16, 128], F32, kind="ExternalInput").ap()
    ident_d = dt("ident", [128, 128], BF16, kind="ExternalInput").ap()
    maskS_d = dt("maskS", [1024, 256], BF16, kind="ExternalInput").ap()
    sel16_d = dt("sel16", [16, 128], BF16, kind="ExternalInput").ap()
    cs_d = [dt(n, [64, 1024], BF16, kind="ExternalInput").ap()
            for n in ("cos0", "sin0", "cos1", "sin1")]

    colld = []
    for li, (krows, vrows, vcols) in enumerate(
            [(2048, 1024, 2048), (1024, 1024, 512), (2048, 1024, 2048)]):
        kl = dt(f"k{li}l", [krows, 1024], BF16)
        kg = dt(f"k{li}g", [2 * krows, 1024], BF16)
        vl = dt(f"v{li}l", [vrows, vcols], BF16)
        vg = dt(f"v{li}g", [2 * vrows, vcols], BF16)
        colld.append((kl, kg, vl, vg))

    sb = nc.alloc_sbuf_tensor
    xs = [sb(f"xs{j}", [128, CC], F32) for j in range(8)]
    lnf = sb("lnf", [128, 16384], BF16)
    attf = sb("attf", [128, 16384], BF16)
    AR = [sb(f"ar{i}", [128, 2048], BF16) for i in range(7)]
    WH = [sb(f"wh{i}", [128, 2048], BF16) for i in range(2)]
    WB = [sb(f"wb{i}", [128, 4096], BF16) for i in range(2)]
    masksb = sb("masksb", [128, 2048], BF16)
    identb = sb("identb", [128, 128], BF16)
    cosT = [sb("cosT0", [64, 1024], BF16), sb("cosT1", [64, 1024], BF16)]
    sinT = [sb("sinT0", [64, 1024], BF16), sb("sinT1", [64, 1024], BF16)]
    onesc = sb("onesc", [128, 1], BF16)
    onesr = sb("onesr", [1, 128], BF16)
    qt_sb = sb("qt_sb", [128, 1024], BF16)
    STG = [sb(f"stg{i}", [128, 1024], BF16) for i in range(2)]
    sqsm = sb("sqsm", [128, 512], BF16)
    avstg = [sb(f"avs{i}", [128, 128], BF16) for i in range(2)]
    rr = {"im": AR[5][0:64, 0:1024], "t1": AR[5][0:64, 1024:2048],
          "t2": AR[6][0:64, 0:1024]}
    kcolbf = sb("kcolbf", [128, 1], BF16)
    knrow = sb("knrow", [1, 128], F32)
    knmax = sb("knmax", [1, 1], F32)
    knmax_bf = sb("knmax_bf", [1, 1], BF16)
    rowsc = sb("rowsc", [1, 16], BF16)
    sel16 = sb("sel16s", [16, 128], BF16)
    t16a = sb("t16a", [16, 256], F32)
    t16b = sb("t16b", [16, 8], F32)
    t16bf = sb("t16bf", [16, 4], BF16)

    sf = lambda n, c: sb(n, [128, c], F32)
    lnsum = sf("lnsum", 8); lnsq = sf("lnsq", 8); lnmean = sf("lnmean", 8)
    lnvar = sf("lnvar", 8); lnms = sf("lnms", 8); lnrstd = sf("lnrstd", 8)
    epsln = sf("epsln", 1); zcol = sf("zcol", 1); negone = sf("negone", 1)
    scstd = sf("scstd", 16); sclow = sf("sclow", 1)
    ulow = sf("ulow", 1); negu = sf("negu", 1)
    lam_b = sf("lam_b", 16); rms08 = sf("rms08", 1)
    neglam = sf("neglam", 16); gamc = sf("gamc", 1); rsA = sf("rsA", 2)
    qn2 = sf("qn2", 16); qns = sf("qns", 16)
    sqk = sf("sqk", 1); sqkb = sf("sqkb", 1)
    scol = sf("scol", 8); kn2 = sf("kn2", 1); s2ch = sf("s2ch", 8)
    btP8 = [sf("btp0", 8), sf("btp1", 8)]
    btpA = sf("btpA", 8); btpB = sf("btpB", 8)
    ssum = sf("ssum", 1); ssum2 = sf("ssum2", 1)
    rs = sf("rs", 1); rs2 = sf("rs2", 1); rsP = sf("rsP", 2); s2P = sf("s2P", 2)
    btile = sf("btile", 1); btile2 = sf("btile2", 1)
    btileP = sf("btileP", 2); btile2P = sf("btile2P", 2)
    m2c = sf("m2c", 1); s2sq = sf("s2sq", 1); den = sf("den", 1); rstdc = sf("rstdc", 1)
    beta = sf("beta", 1)

    psS = [nc.alloc_psum_tensor("psA", [128, 512], F32),
           nc.alloc_psum_tensor("psB", [128, 512], F32)]
    trpX = [nc.alloc_psum_tensor("trpA", [128, 1024], BF16),
            nc.alloc_psum_tensor("trpB", [128, 1024], BF16)]
    qnp = nc.alloc_psum_tensor("qnp", [128, 512], F32)
    avp = nc.alloc_psum_tensor("avp", [128, 256], F32)
    mm0 = nc.alloc_psum_tensor("mm0", [128, 512], F32)
    colp = qnp

    P = Plan(nc)
    for s_ in ["pe", "dv", "ac", "ld", "wd", "kv", "st"] + [f"cc{i}" for i in range(6)]:
        P.sem(s_)

    REL = {}

    def rel(name):
        r = REL.get(name)
        return [r] if r else []

    def dv(fn, waits=(), inc=1):
        P.emit("dv", fn, waits=waits, inc=[("dv", inc)] if inc else [])

    def ac(fn, waits=(), inc=1):
        P.emit("ac", fn, waits=waits, inc=[("ac", inc)] if inc else [])

    def pe(fn, waits=(), inc=1):
        P.emit("pe", fn, waits=waits, inc=[("pe", inc)] if inc else [])

    def sp_dma(dst, src, waits=()):
        P.emit("sp", lambda h, dst=dst, src=src: h.dma_start(out=dst, in_=src),
               waits=waits, inc=[("ld", 16)])

    def gp_dma(dst, src, waits=(), sem="wd"):
        P.emit("gp", lambda h, dst=dst, src=src: h.dma_start(out=dst, in_=src),
               waits=waits, inc=[(sem, 16)])

    dbg_done = set()

    def dbg(name, ap, shape, dtype, waits):
        if not DEV_DBG or name in dbg_done:
            return
        dbg_done.add(name)
        dd = dt("dbg_" + name, list(shape), dtype, kind="ExternalOutput").ap()
        sp_dma(dd[tuple(slice(None) for _ in shape)], ap, waits=waits)

    for j in range(8):
        sp_dma(xs[j][:, :], x_in[j * 128:(j + 1) * 128, :])
    x_ld = [("ld", 16 * (j + 1)) for j in range(8)]
    sp_dma(identb[:, :], ident_d[:, :])
    sp_dma(masksb.ap().rearrange("p (s c) -> p s c", s=8),
           maskS_d.rearrange("(s p) c -> p s c", s=8))
    for a in range(2):
        sp_dma(cosT[a][:, :], cs_d[2 * a][:, :])
        sp_dma(sinT[a][:, :], cs_d[2 * a + 1][:, :])
    sp_dma(t16a[:, 0:128], lq1_d)
    sp_dma(t16a[:, 128:256], lk1_d)
    sp_dma(t16b[:, 0:1], wstd_d)
    sp_dma(t16b[:, 1:2], wrec_d)
    sp_dma(rms08[:, :], rmsw_d)
    sp_dma(sel16[:, :], sel16_d)
    ld_all = [("ld", P.cur("ld"))]

    for ap_, v in [(epsln, 1e-5), (zcol, 0.0), (negone, -1.0)]:
        dv(lambda h, ap_=ap_, v=v: h.memset(ap_[:, :], v), inc=0)
    dv(lambda h: h.memset(onesc[:, :], 1.0), inc=0)
    dv(lambda h: h.memset(onesr[:, :], 1.0), inc=1)

    def edge(eng):
        return [(eng, P.cur(eng))]

    dv(lambda h: h.tensor_scalar(out=rms08[:, :], in0=rms08[:, :], scalar1=0.8,
                                 scalar2=None, op0=ALU.mult), waits=ld_all)
    dv(lambda h: h.tensor_scalar(out=t16b[:, 2:3], in0=t16b[:, 0:1], scalar1=1e-8,
                                 scalar2=None, op0=ALU.max), waits=ld_all)
    dv(lambda h: h.tensor_scalar(out=t16b[:, 3:4], in0=t16b[:, 1:2], scalar1=1e-8,
                                 scalar2=None, op0=ALU.max))
    w = edge("dv")
    for col in (2, 3):
        ac(lambda h, col=col: h.activation(out=t16b[:, col:col + 1], in_=t16b[:, col:col + 1],
                                           func=AF.Ln, bias=zcol[0:16, :], scale=1.0),
           waits=w + edge("ac"))
        w = []
        ac(lambda h, col=col: h.activation(out=t16b[:, col:col + 1], in_=t16b[:, col:col + 1],
                                           func=AF.Exp, bias=zcol[0:16, :], scale=0.5),
           waits=edge("ac"))
    dv(lambda h: h.tensor_scalar(out=t16b[:, 4:5], in0=t16b[:, 1:2], scalar1=0.1,
                                 scalar2=None, op0=ALU.is_gt), waits=ld_all)
    wv_ = edge("ac") + edge("dv")
    dv(lambda h: h.tensor_copy(out=t16bf[:, 0:1], in_=t16b[:, 2:3]), waits=wv_)
    dv(lambda h: h.tensor_copy(out=t16bf[:, 1:2], in_=t16b[:, 3:4]))
    dv(lambda h: h.tensor_copy(out=t16bf[:, 2:3], in_=t16b[:, 4:5]))
    w = edge("dv")
    pe(lambda h: h.transpose(out=trpX[0][0:1, 0:16], in_=t16bf[:, 0:1],
                             identity=identb[0:16, 0:16]), waits=w + ld_all)
    w = edge("pe")
    dv(lambda h: h.tensor_copy(out=rowsc[:, :], in_=trpX[0][0:1, 0:16]), waits=w)
    w = edge("dv")
    pe(lambda h: h.matmul(out=colp[0:120, 0:16], lhsT=onesr[:, 0:120], rhs=rowsc[:, :],
                          start=True, stop=True), waits=w)
    w = edge("pe")
    dv(lambda h: h.tensor_copy(out=scstd[0:120, :], in_=colp[0:120, 0:16]), waits=w)
    w = edge("dv")
    pe(lambda h: h.matmul(out=colp[:, 16:17], lhsT=sel16[:, :], rhs=t16bf[:, 1:2],
                          start=True, stop=True), waits=w)
    pe(lambda h: h.matmul(out=colp[:, 17:18], lhsT=sel16[:, :], rhs=t16bf[:, 2:3],
                          start=True, stop=True))
    w = edge("pe")
    dv(lambda h: h.tensor_copy(out=sclow[:, :], in_=colp[:, 16:17]), waits=w)
    dv(lambda h: h.tensor_copy(out=ulow[:, :], in_=colp[:, 17:18]))
    w = edge("dv")
    dv(lambda h: h.tensor_scalar(out=negu[:, :], in0=ulow[:, :], scalar1=-1.0,
                                 scalar2=None, op0=ALU.mult), waits=w)
    dv(lambda h: h.tensor_tensor(out=t16a[:, 0:128], in0=t16a[:, 0:128], in1=t16a[:, 128:256],
                                 op=ALU.mult), waits=ld_all)
    w = edge("dv")
    dv(lambda h: h.reduce_sum(out=t16b[:, 5:6], in_=t16a[:, 0:128], axis=AX.X), waits=w)
    w = edge("dv")
    sp_dma(t16a[:, 0:128], lq2_d, waits=w)
    sp_dma(t16a[:, 128:256], lk2_d)
    ld2 = [("ld", P.cur("ld"))]
    dv(lambda h: h.tensor_tensor(out=t16a[:, 0:128], in0=t16a[:, 0:128], in1=t16a[:, 128:256],
                                 op=ALU.mult), waits=ld2)
    w = edge("dv")
    dv(lambda h: h.reduce_sum(out=t16b[:, 6:7], in_=t16a[:, 0:128], axis=AX.X), waits=w)
    w = edge("dv")
    ac(lambda h: h.activation(out=t16b[:, 5:6], in_=t16b[:, 5:6], func=AF.Exp,
                              bias=zcol[0:16, :], scale=1.0), waits=w + edge("ac"))
    ac(lambda h: h.activation(out=t16b[:, 6:7], in_=t16b[:, 6:7], func=AF.Exp,
                              bias=zcol[0:16, :], scale=1.0), waits=edge("ac"))
    w = edge("ac")
    dv(lambda h: h.tensor_tensor(out=t16b[:, 5:6], in0=t16b[:, 5:6], in1=t16b[:, 6:7],
                                 op=ALU.subtract), waits=w)
    w = edge("dv")
    dv(lambda h: h.tensor_scalar(out=t16b[:, 5:6], in0=t16b[:, 5:6], scalar1=0.2,
                                 scalar2=None, op0=ALU.add), waits=w)
    w = edge("dv")
    dv(lambda h: h.tensor_copy(out=t16bf[:, 3:4], in_=t16b[:, 5:6]), waits=w)
    w = edge("dv")
    pe(lambda h: h.transpose(out=trpX[0][0:1, 32:48], in_=t16bf[:, 3:4],
                             identity=identb[0:16, 0:16]), waits=w)
    w = edge("pe")
    dv(lambda h: h.tensor_copy(out=rowsc[:, :], in_=trpX[0][0:1, 32:48]), waits=w)
    w = edge("dv")
    pe(lambda h: h.matmul(out=colp[:, 18:34], lhsT=onesr[:, :], rhs=rowsc[:, :],
                          start=True, stop=True), waits=w)
    w = edge("pe")
    dv(lambda h: h.tensor_copy(out=lam_b[:, :], in_=colp[:, 18:34]), waits=w)
    for _t in ("qnp0", "qnp8", "qnp16", "qnp24", "qnpk64", "qnpk80"):
        REL[_t] = ("dv", P.cur("dv"))
    w = edge("dv")
    dv(lambda h: h.tensor_scalar(out=neglam[:, :], in0=lam_b[:, :], scalar1=-1.0,
                                 scalar2=None, op0=ALU.mult), waits=w)
    REL["trp0"] = ("dv", P.cur("dv"))
    REL["trp1"] = ("dv", P.cur("dv"))

    wd_n = [0]

    def wunit(kind, i, dst, src):
        slot = f"{kind}{i}"
        gp_dma(dst, src, waits=rel(slot), sem="wd")
        wd_n[0] += 1
        return wd_n[0] * 16

    wa_r = wa_d.rearrange("(c p) f -> p c f", c=16)
    wp_r = wp_d.rearrange("(c p) f -> p c f", c=16)
    wq_r = wq_d.rearrange("(c p) f -> p c f", c=16)
    wk_r = wk_d.rearrange("(c p) f -> p c f", c=16)
    wv_r = wv_d.rearrange("(c p) f -> p c f", c=16)
    wo_r = wo_d.rearrange("(c p) f -> p c f", c=16)

    tr_i = [0]

    def transpose_batch(tiles, dests, waits, evac=None):
        half = tr_i[0] % 2
        tr_i[0] += 1
        trp = trpX[half]
        wt = list(waits) + rel(f"trp{half}")
        n = len(tiles)
        for t, tile in enumerate(tiles):
            pe(lambda h, tile=tile, t=t, trp=trp: h.transpose(
                out=trp[:, t * 128:(t + 1) * 128], in_=tile, identity=identb[:, :]),
                waits=wt if t == 0 else (), inc=1 if t == n - 1 else 0)
        w2 = edge("pe")
        nd = len(dests)
        for t, d in enumerate(dests):
            src = trp[:, t * 128:(t + 1) * 128] if nd > 1 else trp[:, 0: n * 128]
            if evac is None:
                dv(lambda h, d=d, src=src: h.tensor_copy(out=d, in_=src),
                   waits=w2 if t == 0 else (), inc=1 if t == nd - 1 else 0)
            else:
                dv(lambda h, d=d, src=src: evac(h, d, src),
                   waits=w2 if t == 0 else (), inc=1 if t == nd - 1 else 0)
        REL[f"trp{half}"] = ("dv", P.cur("dv"))

    def lnfc(c):
        return lnf[:, c * 1024:(c + 1) * 1024]

    def build_ln(xdep):
        sqd = AR[2]
        for j in range(8):
            wj = [xdep[j]] + (rel("ar2") if j == 0 else [])
            dv(lambda h, j=j: h.reduce_sum(out=lnsum[:, j:j + 1], in_=xs[j][:, :], axis=AX.X),
               waits=wj, inc=1 if j == 7 else 0)
            ac(lambda h, j=j, sqd=sqd: h.activation(out=sqd[:, :], in_=xs[j][:, :],
                                                    func=AF.Square,
                                                    accum_out=lnsq[:, j:j + 1]),
               waits=wj, inc=1 if j == 7 else 0)
        REL["ar2"] = ("ac", P.cur("ac"))
        w = edge("dv") + edge("ac")
        dv(lambda h: h.tensor_scalar(out=lnmean[:, :], in0=lnsum[:, :], scalar1=1.0 / CC,
                                     scalar2=None, op0=ALU.mult), waits=w)
        w = edge("dv")
        dv(lambda h: h.tensor_tensor(out=lnms[:, :], in0=lnmean[:, :], in1=lnmean[:, :],
                                     op=ALU.mult), waits=w)
        w = edge("dv")
        dv(lambda h: h.scalar_tensor_tensor(out=lnvar[:, :], in0=lnsq[:, :], scalar=1.0 / CC,
                                            in1=lnms[:, :], op0=ALU.mult, op1=ALU.subtract),
           waits=w)
        w = edge("dv")
        ac(lambda h: h.activation(out=lnrstd[:, :], in_=lnvar[:, :], func=AF.Ln,
                                  bias=epsln[:, :], scale=1.0), waits=w)
        w = edge("ac")
        ac(lambda h: h.activation(out=lnrstd[:, :], in_=lnrstd[:, :], func=AF.Exp,
                                  bias=zcol[:, :], scale=-0.5), waits=w)
        w_ac = edge("ac")
        dbg("lnstats", lnsum[:, :], (128, 8), F32, w_ac + edge("dv"))
        dbg("lnsq", lnsq[:, :], (128, 8), F32, w_ac)
        dbg("lnmean", lnmean[:, :], (128, 8), F32, w_ac)
        dbg("lnvar", lnvar[:, :], (128, 8), F32, w_ac)
        dbg("lnrstd", lnrstd[:, :], (128, 8), F32, w_ac)
        for j in range(8):
            lnt = AR[j % 2]
            wj = w_ac + rel(f"ar{j % 2}")
            dv(lambda h, j=j, lnt=lnt: h.tensor_scalar(
                out=lnt[:, :], in0=xs[j][:, :], scalar1=lnmean[:, j:j + 1],
                scalar2=lnrstd[:, j:j + 1], op0=ALU.subtract, op1=ALU.mult), waits=wj)
            w_dvj = edge("dv")
            lnf3 = lnf.ap().rearrange("p (c t) -> p c t", c=16)
            for cb in range(4):
                tiles = [lnt[:, (cb * 4 + t) * 128:(cb * 4 + t + 1) * 128] for t in range(4)]
                dst = lnf3[:, cb * 4:cb * 4 + 4, j * 128:(j + 1) * 128]
                transpose_batch(tiles, [dst], w_dvj + rel("lnf_w") if cb == 0 else rel("lnf_w"))
            REL[f"ar{j % 2}"] = ("pe", P.cur("pe"))
        REL["lnf"] = ("dv", P.cur("dv"))

    def proj_group(psum_ap, lhs_fn, rhs_fn, wthr, extra=()):
        waits = [("wd", wthr)] + list(extra)
        for c in range(16):
            lhs, rhs = lhs_fn(c), rhs_fn(c)
            pe(lambda h, lhs=lhs, rhs=rhs, c=c: h.matmul(
                out=psum_ap, lhsT=lhs, rhs=rhs, start=(c == 0), stop=(c == 15)),
                waits=waits if c == 0 else (), inc=1 if c == 15 else 0)

    def collective(kl, kg, sem_name, wait_ld):
        P.emit("gp", lambda h, kl=kl, kg=kg: h.collective_compute(
            "AllGather", ALU.bypass, replica_groups=GROUPS,
            ins=[kl.ap().opt()], outs=[kg.ap().opt()]),
            waits=[("ld", wait_ld)], inc=[(sem_name, 1)])

    def gather_k(dstT, kgap, rowbase, cwait):
        kg2 = kgap.rearrange("(s r) t -> s r t", s=2)
        for sec in range(2):
            src_rows = kg2[sec, rowbase:rowbase + 128, :]
            for par in range(2):
                dblk = [(0, 3), (1, 2)][sec][par]
                dst = dstT.ap().rearrange("d (a b) -> d a b", a=4)[
                    :, :, dblk * 128:(dblk + 1) * 128]
                src = src_rows.rearrange("r (a q p) -> r a q p", a=4, q=2)[:, :, par, :]
                sp_dma(dst, src, waits=cwait)

    def gather_v(dstT, vgap, colbase, cwait):
        vg5 = vgap.rearrange("(s a q p) f -> p s a q f", s=2, a=4, q=2, p=128)
        for sec in range(2):
            for par in range(2):
                dblk = [(0, 3), (1, 2)][sec][par]
                dst = dstT.ap().rearrange("p (a b) -> p a b", a=4)[
                    :, :, dblk * 128:(dblk + 1) * 128]
                src = vg5[:, sec, :, par, colbase:colbase + 128]
                sp_dma(dst, src, waits=cwait)

    def knorm_chain(Kg_s, kv_ld_w, sqk_dst, kb):
        tk = f"qnpk{kb}"
        for ch in range(4):
            dv(lambda h, Kg_s=Kg_s, ch=ch: h.tensor_tensor(
                out=sqsm[:, :], in0=Kg_s[:, ch * 512:(ch + 1) * 512],
                in1=Kg_s[:, ch * 512:(ch + 1) * 512], op=ALU.mult),
                waits=(kv_ld_w + rel("sqsm") + rel(tk)) if ch == 0 else edge("dv"))
            w2 = edge("dv") + (rel(tk) if ch == 0 else [])
            for t in range(4):
                kc = kb + ch * 4 + t
                pe(lambda h, kc=kc, t=t: h.matmul(
                    out=qnp[:, kc:kc + 1], lhsT=sqsm[:, t * 128:(t + 1) * 128],
                    rhs=onesc[:, :], start=True, stop=True),
                    waits=w2 if t == 0 else (), inc=1 if t == 3 else 0)
            REL["sqsm"] = ("pe", P.cur("pe"))
        w = edge("pe")
        dv(lambda h, kb=kb: h.reduce_max(out=kn2[:, :], in_=qnp[:, kb:kb + 16], axis=AX.X),
           waits=w)
        w = edge("dv")
        dv(lambda h: h.tensor_copy(out=kcolbf[:, :], in_=kn2[:, :]), waits=w)
        w = edge("dv")
        pe(lambda h: h.transpose(out=trpX[0][0:1, 0:128], in_=kcolbf[:, :],
                                 identity=identb[:, :]), waits=w + rel("trp0"))
        w = edge("pe")
        dv(lambda h: h.tensor_copy(out=knrow[:, :], in_=trpX[0][0:1, 0:128]), waits=w)
        REL["trp0"] = ("dv", P.cur("dv"))
        w = edge("dv")
        dv(lambda h: h.reduce_max(out=knmax[:, :], in_=knrow[:, :], axis=AX.X), waits=w)
        w = edge("dv")
        ac(lambda h: h.activation(out=knmax[:, :], in_=knmax[:, :], func=AF.Ln,
                                  bias=epsln[0:1, :], scale=1.0), waits=w + edge("ac"))
        w = edge("ac")
        ac(lambda h: h.activation(out=knmax[:, :], in_=knmax[:, :], func=AF.Exp,
                                  bias=zcol[0:1, :], scale=0.5), waits=w)
        w = edge("ac")
        dv(lambda h: h.tensor_copy(out=knmax_bf[:, :], in_=knmax[:, :]), waits=w)
        w = edge("dv")
        pe(lambda h, kb=kb: h.matmul(out=qnp[:, kb + 32:kb + 33], lhsT=onesr[:, :],
                                     rhs=knmax_bf[:, :], start=True, stop=True), waits=w)
        w = edge("pe")
        dv(lambda h, sqk_dst=sqk_dst, kb=kb: h.tensor_copy(out=sqk_dst[:, :],
                                                           in_=qnp[:, kb + 32:kb + 33]),
           waits=w)
        REL[tk] = ("dv", P.cur("dv"))

    avp_par = [0]

    def attn_tailB(j, Psrc, Vg_s, dco, w_exp, scal_fn, final_scale, vwait=(), w_exp_ch=None):
        PT = AR[6]
        ntb = (LPAD[j] + 3) // 4
        if w_exp_ch is None:
            w_exp_ch = [list(w_exp)] * ntb
        par = avp_par[0] % 2
        avp_par[0] += 1
        avt = avp[:, par * 128:(par + 1) * 128]
        evac_w = []

        def emit_av(tb):
            k0, k1 = tb * 4, min(tb * 4 + 4, LPAD[j])
            for kt in range(k0, k1):
                lhs = PT[:, kt * 128:(kt + 1) * 128]
                rhsap = Vg_s[:, kt * 128:(kt + 1) * 128]
                wv = ()
                if kt == k0:
                    wv = list(evac_w[tb])
                    if tb == 0:
                        wv = wv + rel(f"avt{par}") + list(vwait)
                pe(lambda h, lhs=lhs, rhsap=rhsap, kt=kt, j=j, avt=avt: h.matmul(
                    out=avt, lhsT=lhs, rhs=rhsap, start=(kt == 0), stop=(kt == LPAD[j] - 1)),
                    waits=wv)

        for tb in range(ntb):
            nt = min(4, LPAD[j] - tb * 4)
            tiles = [Psrc[:, (tb * 4 + t) * 128:(tb * 4 + t + 1) * 128] for t in range(nt)]
            dsts = [PT[:, (tb * 4 + t) * 128:(tb * 4 + t + 1) * 128] for t in range(nt)]
            transpose_batch(tiles, dsts,
                            list(w_exp_ch[min(tb, len(w_exp_ch) - 1)]) +
                            (rel("ar6") if tb == 0 else []))
            evac_w.append(edge("dv"))
            if tb >= 1:
                emit_av(tb - 1)
        relP = ("pe", P.cur("pe"))
        emit_av(ntb - 1)
        REL["ar6"] = ("pe", P.cur("pe"))
        w_av = edge("pe")
        scal_ap, w_scal = scal_fn(w_av, avt)
        stgt = avstg[par]
        dv(lambda h, stgt=stgt, scal_ap=scal_ap, avt=avt: h.tensor_scalar(
            out=stgt[:, :], in0=avt, scalar1=scal_ap, scalar2=None, op0=ALU.mult),
            waits=w_av + list(w_scal) + rel(f"avstg{par}"))
        REL[f"avt{par}"] = ("dv", P.cur("dv"))
        w2 = edge("dv")
        if final_scale is None:
            transpose_batch([stgt[:, :]], [dco], w2 + rel("attf_w"))
        else:
            transpose_batch([stgt[:, :]], [dco], w2 + rel("attf_w"),
                            evac=lambda h, d, s, fs=final_scale: h.tensor_scalar(
                                out=d, in0=s, scalar1=fs, scalar2=None, op0=ALU.mult))
        REL[f"avstg{par}"] = ("pe", P.cur("pe"))
        return relP

    chpar = [0]

    def score_chunks(j, lhsT_ap, Kgx, Pdst, bt_ap, bt_dv, scolc, wqk, slot_rel_keys):
        Lp = LPAD[j] * 128
        nch = (Lp + 511) // 512
        exp_ch = []
        for ch in range(nch):
            par = chpar[0] % 2
            chpar[0] += 1
            pss = psS[par]
            c0 = ch * 512
            cw = min(512, Lp - c0)
            hasmask = (c0 + cw == Lp)
            wfirst = list(wqk) + rel(f"psS{par}")
            pe(lambda h, pss=pss, cw=cw, lhsT_ap=lhsT_ap, c0=c0, hm=hasmask:
               h.matmul(out=pss[:, 0:cw], lhsT=lhsT_ap, rhs=Kgx[:, c0:c0 + cw],
                        start=True, stop=not hm),
                waits=wfirst, inc=0 if hasmask else 1)
            if hasmask:
                moff = cw - 256
                pe(lambda h, pss=pss, moff=moff, j=j: h.matmul(
                    out=pss[:, moff:moff + 256], lhsT=identb[:, :],
                    rhs=masksb[:, j * 256:(j + 1) * 256], start=False, stop=True))
            w2 = edge("pe")
            ac(lambda h, cw=cw, c0=c0, pss=pss, Pdst=Pdst, bt_ap=bt_ap, ch=ch, scolc=scolc:
               h.activation(out=Pdst[:, c0:c0 + cw], in_=pss[:, 0:cw], func=AF.Exp,
                            bias=bt_ap, scale=RSQD,
                            accum_out=scol[:, scolc + ch:scolc + ch + 1]),
                waits=w2 + list(bt_dv) +
                (([rel(k)[0] for k in slot_rel_keys if REL.get(k)] + rel("scol"))
                 if ch == 0 else []))
            REL[f"psS{par}"] = ("ac", P.cur("ac"))
            exp_ch.append(edge("ac"))
        return nch, exp_ch

    def qnorm_chain(Qsrc, width, qcols, qld_w):
        nch = width // 512
        q0 = qcols[0]
        tq = f"qnp{q0}"
        for ch in range(nch):
            dv(lambda h, Qsrc=Qsrc, ch=ch: h.tensor_tensor(
                out=sqsm[:, :], in0=Qsrc[:, ch * 512:(ch + 1) * 512],
                in1=Qsrc[:, ch * 512:(ch + 1) * 512], op=ALU.mult),
                waits=(qld_w + rel("sqsm") + rel(tq)) if ch == 0 else edge("dv"))
            w2 = edge("dv") + (rel(tq) if ch == 0 else [])
            for t in range(4):
                cc_ = q0 + ch * 4 + t
                pe(lambda h, cc_=cc_, t=t: h.matmul(
                    out=qnp[:, cc_:cc_ + 1], lhsT=sqsm[:, t * 128:(t + 1) * 128],
                    rhs=onesc[:, :], start=True, stop=True),
                    waits=w2 if t == 0 else (), inc=1 if t == 3 else 0)
            REL["sqsm"] = ("pe", P.cur("pe"))
        w = edge("pe")
        c0, c1 = qcols[0], qcols[0] + width // 128
        dv(lambda h, c0=c0, c1=c1: h.tensor_copy(out=qn2[:, c0:c1], in_=qnp[:, c0:c1]), waits=w)
        REL[tq] = ("dv", P.cur("dv"))
        w = edge("dv")
        ac(lambda h, c0=c0, c1=c1: h.activation(out=qns[:, c0:c1], in_=qn2[:, c0:c1],
                                                func=AF.Ln, bias=epsln[:, :], scale=1.0),
           waits=w + edge("ac"))
        w = edge("ac")
        ac(lambda h, c0=c0, c1=c1: h.activation(out=qns[:, c0:c1], in_=qns[:, c0:c1],
                                                func=AF.Exp, bias=zcol[:, :], scale=0.5),
           waits=w)

    def build_ffn(li, xdep):
        kl, kg, vl, vg = colld[li]
        ccK, ccV = f"cc{li * 2}", f"cc{li * 2 + 1}"
        build_ln(xdep)
        lnf_rel = [REL["lnf"]]
        dbg("lnf", lnf[:, :], (128, 16384), BF16, [REL["lnf"]])

        for h_ in range(16):
            si = h_ % 2
            thr = wunit("wh", si,
                        WH[si][:, 0:1920].rearrange("p (c f) -> p c f", c=16),
                        wa_r[:, :, 1920 + h_ * DS: 1920 + (h_ + 1) * DS])
            kst = STG[si]
            scl = scstd[0:120, h_:h_ + 1]
            for th in range(2):
                psu = mm0[0:120, 0:512]
                proj_group(psu,
                           lambda c, si=si: WH[si][:, c * DS:(c + 1) * DS],
                           lambda c, th=th: lnfc(c)[:, th * 512:(th + 1) * 512],
                           thr, extra=lnf_rel + rel("mm0") + (rel(f"stg{si}") if th == 0 else []))
                w2 = edge("pe")
                ac(lambda h, kst=kst, th=th, psu=psu, scl=scl: h.activation(
                    out=kst[0:120, th * 512:(th + 1) * 512], in_=psu, func=AF.Copy,
                    bias=0.0, scale=scl), waits=w2)
                REL["mm0"] = ("ac", P.cur("ac"))
            REL[f"wh{si}"] = ("pe", P.cur("pe"))
            w2 = edge("ac")
            sp_dma(kl.ap()[h_ * 128: h_ * 128 + 120, :], kst[0:120, :], waits=w2)
            REL[f"stg{si}"] = ("ld", P.cur("ld"))
        qlo, klo, dlo = STG[0], STG[1], AR[4][:, 0:1024]
        for part, dst in [(0, qlo), (1, klo)]:
            si = part
            thr = wunit("wh", si, WH[si][:, 0:2048].rearrange("p (c f) -> p c f", c=16),
                        wa_r[:, :, 5888 + part * 128: 5888 + (part + 1) * 128])
            for th in range(2):
                proj_group(mm0[:, 0:512],
                           lambda c, si=si: WH[si][:, c * 128:(c + 1) * 128],
                           lambda c, th=th: lnfc(c)[:, th * 512:(th + 1) * 512],
                           thr, extra=lnf_rel + rel("mm0") +
                           (rel(f"stg{part}") if th == 0 else []))
                w2 = edge("pe")
                ac(lambda h, dst=dst, th=th: h.activation(
                    out=dst[:, th * 512:(th + 1) * 512], in_=mm0[:, 0:512], func=AF.Copy,
                    bias=0.0, scale=sclow[:, :]), waits=w2)
                REL["mm0"] = ("ac", P.cur("ac"))
            REL[f"wh{si}"] = ("pe", P.cur("pe"))
        w = edge("ac") + rel("ar4")
        dv(lambda h: h.tensor_tensor(out=dlo, in0=klo[:, :], in1=qlo[:, :],
                                     op=ALU.subtract), waits=w)
        w = edge("dv")
        dv(lambda h: h.scalar_tensor_tensor(out=qt_sb[:, :], in0=dlo, scalar=ulow[:, :],
                                            in1=qlo[:, :], op0=ALU.mult, op1=ALU.add),
           waits=w + rel("qt_sb"))
        dv(lambda h: h.scalar_tensor_tensor(out=dlo, in0=dlo, scalar=negu[:, :],
                                            in1=klo[:, :], op0=ALU.mult, op1=ALU.add))
        w = edge("dv")
        REL["stg0"] = ("dv", P.cur("dv"))
        REL["stg1"] = ("dv", P.cur("dv"))
        sp_dma(kl.ap().rearrange("(hh d) t -> hh d t", hh=16)[:, 120:128, :],
               dlo, waits=w)
        REL["ar4"] = ("ld", P.cur("ld"))
        vthr = {}
        for vc in (0, 1):
            vthr[vc] = wunit("wh", vc % 2,
                             WH[vc % 2][:, 0:2048].rearrange("p (c f) -> p c f", c=16),
                             wa_r[:, :, 3840 + vc * 128: 3840 + (vc + 1) * 128])
        if EARLY_CC:
            collective(kl, kg, ccK, P.cur("ld"))
        for vc in range(16):
            si = vc % 2
            thr = vthr[vc] if vc in vthr else wunit(
                "wh", si, WH[si][:, 0:2048].rearrange("p (c f) -> p c f", c=16),
                wa_r[:, :, 3840 + vc * 128: 3840 + (vc + 1) * 128])
            vstg = STG[si]
            for th in range(2):
                proj_group(mm0[:, 0:512],
                           lambda c, si=si: WH[si][:, c * 128:(c + 1) * 128],
                           lambda c, th=th: lnfc(c)[:, th * 512:(th + 1) * 512],
                           thr, extra=lnf_rel + rel("mm0") +
                           (rel(f"stg{si}") if th == 0 else []))
                w2 = edge("pe")
                ac(lambda h, vstg=vstg, th=th: h.activation(
                    out=vstg[:, th * 512:(th + 1) * 512], in_=mm0[:, 0:512], func=AF.Copy,
                    bias=0.0, scale=1.0), waits=w2)
                REL["mm0"] = ("ac", P.cur("ac"))
            REL[f"wh{si}"] = ("pe", P.cur("pe"))
            w2 = edge("ac")
            for jb in range(2):
                tiles = [vstg[:, (jb * 4 + t) * 128:(jb * 4 + t + 1) * 128] for t in range(4)]
                dsts = [AR[3][:, (jb * 4 + t) * 128:(jb * 4 + t + 1) * 128] for t in range(4)]
                transpose_batch(tiles, dsts, (w2 + rel("ar3")) if jb == 0 else [])
            REL[f"stg{si}"] = ("pe", P.cur("pe"))
            w2 = edge("dv")
            sp_dma(vl.ap().rearrange("(jt p) f -> p jt f", jt=8)[:, :, vc * 128:(vc + 1) * 128],
                   AR[3].ap()[:, 0:1024].rearrange("p (jt f) -> p jt f", jt=8), waits=w2)
            REL["ar3"] = ("ld", P.cur("ld"))
        dbg("qt_sb", qt_sb[:, :], (128, 1024), BF16, [("dv", P.cur("dv"))])
        qpre = {}
        for hh in (0, 1):
            qpre[hh] = wunit("wh", hh % 2,
                             WH[hh % 2][:, 0:1920].rearrange("p (c f) -> p c f", c=16),
                             wa_r[:, :, hh * DS:(hh + 1) * DS])
        kvw = P.cur("ld")
        dbg("k1l", kl.ap()[:, :], tuple(kl.shape), BF16, [("ld", kvw)])
        dbg("v1l", vl.ap()[:, :], tuple(vl.shape), BF16, [("ld", kvw)])
        if not EARLY_CC:
            collective(kl, kg, ccK, kvw)
        collective(vl, vg, ccV, kvw)

        def ffn_stageA1(j, qs):
            jp = j % 2
            Pun = AR[4 + jp]
            lhsT_ap = qs["Qf"][:, j * 128:(j + 1) * 128]
            nch, exp_ch = score_chunks(j, lhsT_ap, qs["Kg"], Pun, qs["btp"][:, j:j + 1],
                                       qs["bt_dv"], 0,
                                       list(qs["kld"]) + list(qs["qf_ld"]), [f"ar{4 + jp}"])
            return {"exp": edge("ac"), "exp_ch": exp_ch, "Pun": Pun, "jp": jp, "nch": nch}

        def ffn_stageA2(j, st):
            jp = st["jp"]
            tok_exp = st["exp"]
            nch = st["nch"]
            dv(lambda h, nch=nch: h.reduce_sum(out=ssum[:, :], in_=scol[:, 0:nch],
                                               axis=AX.X), waits=tok_exp + rel("ssum"))
            REL["scol"] = ("dv", P.cur("dv"))
            w2 = edge("dv")
            dv(lambda h, jp=jp: h.reciprocal(out=rsP[:, jp:jp + 1], in_=ssum[:, :]), waits=w2)
            REL["ssum"] = ("dv", P.cur("dv"))
            st["rs"] = edge("dv")
            return st

        def ffn_stageB(j, st, Vg_s, h_, vld):
            jp = st["jp"]
            dco = attf[:, h_ * 1024 + j * 128: h_ * 1024 + (j + 1) * 128]

            def scal_fn(w_av, avt_t, jp=jp, st=st):
                return rsP[:, jp:jp + 1], st["rs"]

            relP = attn_tailB(j, st["Pun"], Vg_s, dco, st["exp"], scal_fn, None, vwait=vld,
                              w_exp_ch=st["exp_ch"])
            REL[f"ar{4 + jp}"] = relP

        def q_parts(h2):
            si = h2 % 2
            qb = 8 * si
            sqk_dst = sqk if si == 0 else sqkb
            kb = 64 if si == 0 else 80
            qst = {"Qf": STG[si], "Kg": AR[0 + 2 * si], "Vg": AR[1 + 2 * si],
                   "qb": qb, "sqk": sqk_dst}
            scl = scstd[0:120, h2:h2 + 1]

            def p_w():
                if h2 in qpre:
                    qst["thr"] = qpre[h2]
                else:
                    qst["thr"] = wunit("wh", si,
                                       WH[si][:, 0:1920].rearrange("p (c f) -> p c f", c=16),
                                       wa_r[:, :, h2 * DS:(h2 + 1) * DS])

            def p_proj():
                Qf = qst["Qf"]
                for th in range(2):
                    proj_group(mm0[0:120, 0:512],
                               lambda c, si=si: WH[si][:, c * DS:(c + 1) * DS],
                               lambda c, th=th: lnfc(c)[:, th * 512:(th + 1) * 512],
                               qst["thr"], extra=lnf_rel + rel("mm0") +
                               (rel(f"stg{si}") if th == 0 else []))
                    w2 = edge("pe")
                    ac(lambda h, Qf=Qf, th=th, scl=scl: h.activation(
                        out=Qf[0:120, th * 512:(th + 1) * 512], in_=mm0[0:120, 0:512],
                        func=AF.Copy, bias=0.0, scale=scl), waits=w2)
                    REL["mm0"] = ("ac", P.cur("ac"))
                REL[f"wh{si}"] = ("pe", P.cur("pe"))
                w2 = edge("ac")
                sp_dma(Qf[120:128, :], qt_sb[h2 * 8:(h2 + 1) * 8, :], waits=w2)
                qst["qf_ld"] = [("ld", P.cur("ld"))]

            def p_gk():
                gather_k(qst["Kg"], kg.ap(), h2 * 128, [(ccK, 1)] + rel(f"ar{0 + 2 * si}"))
                qst["kld"] = [("ld", P.cur("ld"))]

            def p_gv():
                gather_v(qst["Vg"], vg.ap(), h2 * 128, [(ccV, 1)] + rel(f"ar{1 + 2 * si}"))
                qst["vld"] = [("ld", P.cur("ld"))]

            def p_qn():
                qnorm_chain(qst["Qf"], 1024, list(range(qb, qb + 8)), qst["qf_ld"])
                qst["qns_ac"] = edge("ac")

            def p_kn():
                knorm_chain(qst["Kg"], qst["kld"], sqk_dst, kb)
                qst["sqk_dv"] = edge("dv")

            return qst, p_w, p_proj, p_gk, p_gv, p_qn, p_kn

        st_q = {}
        pend = {}
        pa = q_parts(0)
        pb = q_parts(1)
        pa[1](); pa[2]()
        pb[1](); pb[2]()
        pa[3](); pa[5](); pa[6]()
        pb[3](); pb[5](); pb[6]()
        pa[4](); pb[4]()
        st_q[0] = pa[0]
        st_q[1] = pb[0]

        for h_ in range(16):
            si = h_ % 2
            qs = st_q.pop(h_)
            Vg_s = qs["Vg"]
            vld = qs["vld"]
            nxt = q_parts(h_ + 2) if h_ + 2 < 16 else None
            pq = pend.pop(h_ + 1, None)
            btp = btP8[si]
            qb = qs["qb"]
            sqkh = qs["sqk"]
            dv(lambda h, btp=btp, qb=qb, sqkh=sqkh: h.tensor_scalar(
                out=btp[:, :], in0=qns[:, qb:qb + 8], scalar1=sqkh[:, :], scalar2=-RSQD,
                op0=ALU.mult, op1=ALU.mult),
               waits=list(qs["sqk_dv"]) + list(qs["qns_ac"]) + rel(f"btp{si}"))
            qs["btp"] = btp
            qs["bt_dv"] = edge("dv")
            stA = {}
            for j in range(8):
                a1 = ffn_stageA1(j, qs)
                if j == 0 and nxt is not None:
                    nxt[1]()
                if j == 2 and pq is not None:
                    pq[0]()
                if j == 4 and pq is not None:
                    pq[1]()
                if j > 0:
                    ffn_stageB(j - 1, stA[j - 1], Vg_s, h_, vld)
                stA[j] = ffn_stageA2(j, a1)
            REL[f"btp{si}"] = ("ac", P.cur("ac"))
            REL[f"stg{si}"] = ("pe", P.cur("pe"))
            REL[f"ar{0 + 2 * si}"] = ("pe", P.cur("pe"))
            if nxt is not None:
                nxt[2]()
                nxt[3]()
            ffn_stageB(7, stA[7], Vg_s, h_, vld)
            REL[f"ar{1 + 2 * si}"] = ("pe", P.cur("pe"))
            if nxt is not None:
                nxt[4]()
                pend[h_ + 2] = (nxt[5], nxt[6])
                st_q[h_ + 2] = nxt[0]
        REL["qt_sb"] = ("ld", P.cur("ld"))
        attf_dv = [("dv", P.cur("dv"))]
        dbg("attf", attf[:, :], (128, 16384), BF16, attf_dv)

        resid_tok = [None] * 8
        for cb in range(4):
            thr0 = wunit("wb", 0, WB[0][:, :].rearrange("p (c f) -> p c f", c=8),
                         wp_r[:, 0:8, cb * 512:(cb + 1) * 512])
            thr1 = wunit("wb", 1, WB[1][:, :].rearrange("p (c f) -> p c f", c=8),
                         wp_r[:, 8:16, cb * 512:(cb + 1) * 512])
            for j in range(8):
                for h_ in range(16):
                    si = h_ // 8
                    lhs = attf[:, h_ * 1024 + j * 128: h_ * 1024 + (j + 1) * 128]
                    rhsap = WB[si][:, (h_ % 8) * 512:((h_ % 8) + 1) * 512]
                    wt = ()
                    if h_ == 0:
                        wt = [("wd", thr0)] + attf_dv + rel("mm0")
                    elif h_ == 8:
                        wt = [("wd", thr1)]
                    pe(lambda h, lhs=lhs, rhsap=rhsap, h_=h_: h.matmul(
                        out=mm0[:, 0:512], lhsT=lhs, rhs=rhsap,
                        start=(h_ == 0), stop=(h_ == 15)), waits=wt,
                        inc=1 if h_ == 15 else 0)
                w2 = edge("pe")
                xt = xs[j][:, cb * 512:(cb + 1) * 512]
                dv(lambda h, xt=xt: h.scalar_tensor_tensor(
                    out=xt, in0=mm0[:, 0:512], scalar=0.5, in1=xt,
                    op0=ALU.mult, op1=ALU.add), waits=w2)
                REL["mm0"] = ("dv", P.cur("dv"))
                if cb == 3:
                    resid_tok[j] = ("dv", P.cur("dv"))
            REL["wb0"] = ("pe", P.cur("pe"))
            REL["wb1"] = ("pe", P.cur("pe"))
        REL["attf_w"] = ("pe", P.cur("pe"))
        REL["lnf_w"] = ("pe", P.cur("pe"))
        return resid_tok

    def build_attn(xdep):
        kl, kg, vl, vg = colld[1]
        ccK, ccV = "cc2", "cc3"
        build_ln(xdep)
        lnf_rel = [REL["lnf"]]

        def rope(pre, a, w_pre, dre=None):
            sp_dma(rr["im"], pre[64:128, :],
                   waits=list(w_pre) + [("dv", P.cur("dv"))] + rel("ar5"))
            im_ld = [("ld", P.cur("ld"))]
            ca, sa = cosT[a], sinT[a]
            dv(lambda h, pre=pre, ca=ca: h.tensor_tensor(
                out=rr["t1"], in0=pre[0:64, :], in1=ca[:, :], op=ALU.mult),
                waits=list(w_pre) + rel("rope") + rel("ar5") + rel("ar6"))
            dv(lambda h, sa=sa: h.tensor_tensor(
                out=rr["t2"], in0=rr["im"], in1=sa[:, :], op=ALU.mult),
                waits=im_ld)
            w2 = edge("dv")
            ore_dst = dre if dre is not None else rr["t1"]
            dv(lambda h, ore_dst=ore_dst: h.tensor_tensor(
                out=ore_dst, in0=rr["t1"], in1=rr["t2"], op=ALU.subtract),
                waits=w2)
            dv(lambda h, pre=pre, sa=sa: h.tensor_tensor(
                out=rr["t2"], in0=pre[0:64, :], in1=sa[:, :], op=ALU.mult))
            dv(lambda h, ca=ca: h.tensor_tensor(
                out=rr["im"], in0=rr["im"], in1=ca[:, :], op=ALU.mult))
            w2 = edge("dv")
            dv(lambda h: h.tensor_tensor(out=rr["t2"], in0=rr["t2"],
                                         in1=rr["im"], op=ALU.add), waits=w2)
            return edge("dv")

        for kvh in range(4):
            for a in range(2):
                si = (kvh * 2 + a) % 2
                thr = wunit("wh", si, WH[si][:, 0:2048].rearrange("p (c f) -> p c f", c=16),
                            wk_r[:, :, (kvh * 2 + a) * 128:(kvh * 2 + a + 1) * 128])
                pre = STG[si]
                for th in range(2):
                    proj_group(mm0[:, 0:512],
                               lambda c, si=si: WH[si][:, c * 128:(c + 1) * 128],
                               lambda c, th=th: lnfc(c)[:, th * 512:(th + 1) * 512],
                               thr, extra=lnf_rel + rel("mm0") +
                               (rel(f"stg{si}") if th == 0 else []))
                    w2 = edge("pe")
                    ac(lambda h, pre=pre, th=th: h.activation(
                        out=pre[:, th * 512:(th + 1) * 512], in_=mm0[:, 0:512],
                        func=AF.Copy, bias=0.0, scale=1.0), waits=w2)
                    REL["mm0"] = ("ac", P.cur("ac"))
                REL[f"wh{si}"] = ("pe", P.cur("pe"))
                w_pre = edge("ac")
                wr = rope(pre, a, w_pre)
                REL[f"stg{si}"] = ("dv", P.cur("dv"))
                row0 = (kvh * 2 + a) * 128
                sp_dma(kl.ap()[row0:row0 + 64, :], rr["t1"], waits=wr)
                sp_dma(kl.ap()[row0 + 64:row0 + 128, :], rr["t2"], waits=wr)
                REL["rope"] = ("ld", P.cur("ld"))
                REL["ar5"] = ("ld", P.cur("ld"))
                REL["ar6"] = ("ld", P.cur("ld"))
        vthr = {}
        for vc in (0, 1):
            vthr[vc] = wunit("wh", vc % 2,
                             WH[vc % 2][:, 0:2048].rearrange("p (c f) -> p c f", c=16),
                             wv_r[:, :, vc * 128:(vc + 1) * 128])
        if EARLY_CC:
            collective(kl, kg, ccK, P.cur("ld"))
        for vc in range(4):
            si = vc % 2
            thr = vthr[vc] if vc in vthr else wunit(
                "wh", si, WH[si][:, 0:2048].rearrange("p (c f) -> p c f", c=16),
                wv_r[:, :, vc * 128:(vc + 1) * 128])
            vstg = STG[si]
            for th in range(2):
                proj_group(mm0[:, 0:512],
                           lambda c, si=si: WH[si][:, c * 128:(c + 1) * 128],
                           lambda c, th=th: lnfc(c)[:, th * 512:(th + 1) * 512],
                           thr, extra=lnf_rel + rel("mm0") +
                           (rel(f"stg{si}") if th == 0 else []))
                w2 = edge("pe")
                ac(lambda h, vstg=vstg, th=th: h.activation(
                    out=vstg[:, th * 512:(th + 1) * 512], in_=mm0[:, 0:512], func=AF.Copy,
                    bias=0.0, scale=1.0), waits=w2)
                REL["mm0"] = ("ac", P.cur("ac"))
            REL[f"wh{si}"] = ("pe", P.cur("pe"))
            w2 = edge("ac")
            for jb in range(2):
                tiles = [vstg[:, (jb * 4 + t) * 128:(jb * 4 + t + 1) * 128] for t in range(4)]
                dsts = [AR[3][:, (jb * 4 + t) * 128:(jb * 4 + t + 1) * 128] for t in range(4)]
                transpose_batch(tiles, dsts, (w2 + rel("ar3")) if jb == 0 else [])
            REL[f"stg{si}"] = ("pe", P.cur("pe"))
            w2 = edge("dv")
            sp_dma(vl.ap().rearrange("(jt p) f -> p jt f", jt=8)[:, :, vc * 128:(vc + 1) * 128],
                   AR[3].ap()[:, 0:1024].rearrange("p (jt f) -> p jt f", jt=8), waits=w2)
            REL["ar3"] = ("ld", P.cur("ld"))
        qpre = {}
        for aa in (0, 1):
            qpre[(0, aa)] = wunit("wh", aa % 2,
                                  WH[aa % 2][:, 0:2048].rearrange("p (c f) -> p c f", c=16),
                                  wq_r[:, :, aa * 128:(aa + 1) * 128])
        if not EARLY_CC:
            collective(kl, kg, ccK, P.cur("ld"))
        collective(vl, vg, ccV, P.cur("ld"))

        PEX = [WB[0][:, 0:2048], WB[1][:, 0:2048]]

        def attn_stageA1(j, Qr, Kg0, Kg1, kv_ld, q_ready, bt_dv, h_):
            jp = j % 2
            P0, P1 = AR[4], AR[5]
            score_chunks(j, Qr[:, j * 128:(j + 1) * 128], Kg0, P0, btpA[:, j:j + 1],
                         bt_dv, 0, list(kv_ld) + list(q_ready), ["ar4"])
            score_chunks(j, Qr[:, 1024 + j * 128: 1024 + (j + 1) * 128], Kg1, P1,
                         btpB[:, j:j + 1], bt_dv, 4, list(kv_ld) + list(q_ready), ["ar5"])
            return {"exp1": edge("ac"), "jp": jp}

        def attn_stageA2(j, st, h_):
            jp = st["jp"]
            Lp = LPAD[j] * 128
            P0, P1 = AR[4], AR[5]
            tok_exp = st["exp1"]
            nch = (Lp + 511) // 512
            dv(lambda h, nch=nch: h.reduce_sum(out=ssum[:, :], in_=scol[:, 0:nch],
                                               axis=AX.X), waits=tok_exp + rel("ssum"))
            dv(lambda h, nch=nch: h.reduce_sum(out=ssum2[:, :], in_=scol[:, 4:4 + nch],
                                               axis=AX.X))
            REL["scol"] = ("dv", P.cur("dv"))
            w2 = edge("dv")
            if FUSE_COMB:
                dv(lambda h, jp=jp: h.reciprocal(out=rsA[:, jp:jp + 1], in_=ssum[:, :]),
                   waits=w2 + rel(f"rsA{jp}"))
                dv(lambda h: h.reciprocal(out=rs2[:, :], in_=ssum2[:, :]))
                w2 = edge("dv")
                nlh = neglam[:, h_:h_ + 1]
                dv(lambda h, nlh=nlh: h.tensor_scalar(out=gamc[:, :], in0=ssum[:, :],
                                                      scalar1=rs2[:, :], scalar2=nlh,
                                                      op0=ALU.mult, op1=ALU.mult), waits=w2)
                w2 = edge("dv")
                exp2_ch = []
                for ch in range(nch):
                    c0 = ch * 512
                    cw = min(512, Lp - c0)
                    dv(lambda h, c0=c0, cw=cw: h.scalar_tensor_tensor(
                        out=P0[:, c0:c0 + cw], in0=P1[:, c0:c0 + cw], scalar=gamc[:, :],
                        in1=P0[:, c0:c0 + cw], op0=ALU.mult, op1=ALU.add),
                        waits=w2 if ch == 0 else ())
                    if c0 + cw == Lp:
                        dv(lambda h, c0=c0, cw=cw, j=j: h.tensor_tensor(
                            out=P0[:, c0 + cw - 256:c0 + cw],
                            in0=P0[:, c0 + cw - 256:c0 + cw],
                            in1=masksb[:, j * 256:(j + 1) * 256], op=ALU.add))
                    w3 = edge("dv")
                    ac(lambda h, c0=c0, cw=cw, jp=jp, ch=ch: h.activation(
                        out=PEX[jp][:, c0:c0 + cw], in_=P0[:, c0:c0 + cw],
                        func=AF.Exp, bias=negone[:, :], scale=rsA[:, jp:jp + 1],
                        accum_out=s2ch[:, jp * 4 + ch:jp * 4 + ch + 1]),
                       waits=w3 + (rel(f"wb{jp}") if ch == 0 else []))
                    exp2_ch.append(edge("ac"))
                REL["ar5"] = ("dv", P.cur("dv"))
                REL["ar4"] = ("ac", P.cur("ac"))
                REL[f"rsA{jp}"] = ("ac", P.cur("ac"))
                w4 = edge("ac")
                dv(lambda h, jp=jp, nch=nch: h.reduce_sum(
                    out=s2P[:, jp:jp + 1], in_=s2ch[:, jp * 4:jp * 4 + nch], axis=AX.X),
                    waits=w4)
                REL["ssum"] = ("dv", P.cur("dv"))
                return {"exp": w4, "exp_ch": exp2_ch, "jp": jp}
            else:
                dv(lambda h: h.reciprocal(out=rs[:, :], in_=ssum[:, :]), waits=w2)
                dv(lambda h: h.reciprocal(out=rs2[:, :], in_=ssum2[:, :]))
                w2 = edge("dv")
                lamh = lam_b[:, h_:h_ + 1]
                dv(lambda h, lamh=lamh: h.tensor_scalar(out=beta[:, :], in0=rs2[:, :],
                                                        scalar1=lamh, scalar2=-1.0,
                                                        op0=ALU.mult, op1=ALU.mult), waits=w2)
                w2 = edge("dv")
                dv(lambda h, Lp=Lp: h.tensor_scalar(out=P0[:, 0:Lp], in0=P0[:, 0:Lp],
                                                    scalar1=rs[:, :], scalar2=None,
                                                    op0=ALU.mult), waits=w2)
                dv(lambda h, Lp=Lp: h.tensor_scalar(out=P1[:, 0:Lp], in0=P1[:, 0:Lp],
                                                    scalar1=beta[:, :], scalar2=None,
                                                    op0=ALU.mult))
                w2 = edge("dv")
                dv(lambda h, Lp=Lp: h.tensor_tensor(out=P0[:, 0:Lp], in0=P0[:, 0:Lp],
                                                    in1=P1[:, 0:Lp], op=ALU.add), waits=w2)
                REL["ar5"] = ("dv", P.cur("dv"))
                expsc = 1.0
            w2 = edge("dv")
            dv(lambda h, Lp=Lp, j=j: h.tensor_tensor(
                out=P0[:, Lp - 256:Lp], in0=P0[:, Lp - 256:Lp],
                in1=masksb[:, j * 256:(j + 1) * 256], op=ALU.add), waits=w2)
            w2 = edge("dv")
            ac(lambda h, Lp=Lp, jp=jp, expsc=expsc: h.activation(
                out=PEX[jp][:, 0:Lp], in_=P0[:, 0:Lp],
                func=AF.Exp, bias=negone[:, :], scale=expsc,
                accum_out=s2P[:, jp:jp + 1]),
               waits=w2 + rel(f"wb{jp}"))
            if FUSE_COMB:
                REL[f"rsA{jp}"] = ("ac", P.cur("ac"))
            REL["ar4"] = ("ac", P.cur("ac"))
            REL["ssum"] = ("dv", P.cur("dv"))
            return {"exp": edge("ac"), "jp": jp}

        def attn_stageB(j, st, Vg_s, h_, vld):
            jp = st["jp"]
            dco = attf[:, h_ * 1024 + j * 128: h_ * 1024 + (j + 1) * 128]

            def scal_fn(w_av, avt_t, jp=jp, st=st):
                ac(lambda h, avt_t=avt_t: h.activation(out=sqsm[:, 0:128], in_=avt_t,
                                                       func=AF.Square,
                                                       accum_out=m2c[:, :]),
                   waits=list(w_av) + rel("sqsm") + list(st["exp"]) + edge("dv"))
                REL["sqsm"] = ("ac", P.cur("ac"))
                w2 = edge("ac")
                dv(lambda h, jp=jp: h.tensor_scalar(out=s2sq[:, :], in0=s2P[:, jp:jp + 1],
                                                    scalar1=s2P[:, jp:jp + 1], scalar2=1e-8,
                                                    op0=ALU.mult, op1=ALU.mult), waits=w2)
                w2 = edge("dv")
                dv(lambda h: h.scalar_tensor_tensor(out=den[:, :], in0=m2c[:, :],
                                                    scalar=1.0 / 128.0, in1=s2sq[:, :],
                                                    op0=ALU.mult, op1=ALU.add), waits=w2)
                w2 = edge("dv")
                ac(lambda h: h.activation(out=rstdc[:, :], in_=den[:, :], func=AF.Ln,
                                          bias=zcol[:, :], scale=1.0), waits=w2 + edge("ac"))
                w2 = edge("ac")
                ac(lambda h: h.activation(out=rstdc[:, :], in_=rstdc[:, :], func=AF.Exp,
                                          bias=zcol[:, :], scale=-0.5), waits=w2)
                return rstdc[:, :], edge("ac")

            relP = attn_tailB(j, PEX[jp], Vg_s, dco, st["exp"], scal_fn, rms08[:, :],
                              vwait=vld, w_exp_ch=st.get("exp_ch"))
            REL[f"wb{jp}"] = relP

        Qr, Kg0, Kg1, Vg_s = AR[3], AR[0], AR[1], AR[2]
        kvst = {}

        def attn_qstage(h2):
            kvh2 = h2 // 4
            st = {}
            for a in range(2):
                sia = a % 2
                key = (h2, a)
                if key in qpre:
                    thr = qpre[key]
                else:
                    thr = wunit("wh", sia,
                                WH[sia][:, 0:2048].rearrange("p (c f) -> p c f", c=16),
                                wq_r[:, :, h2 * 256 + a * 128: h2 * 256 + (a + 1) * 128])
                pre = STG[sia]
                si = sia
                for th in range(2):
                    proj_group(mm0[:, 0:512],
                               lambda c, si=si: WH[si][:, c * 128:(c + 1) * 128],
                               lambda c, th=th: lnfc(c)[:, th * 512:(th + 1) * 512],
                               thr, extra=lnf_rel + rel("mm0") +
                               (rel(f"stg{si}") if th == 0 else []))
                    w2 = edge("pe")
                    ac(lambda h, pre=pre, th=th: h.activation(
                        out=pre[:, th * 512:(th + 1) * 512], in_=mm0[:, 0:512],
                        func=AF.Copy, bias=0.0, scale=1.0), waits=w2)
                    REL["mm0"] = ("ac", P.cur("ac"))
                w_pre = edge("ac") + (rel("ar3") if a == 0 else [])
                wr = rope(pre, a, w_pre, dre=Qr[0:64, a * 1024:(a + 1) * 1024])
                REL[f"stg{si}"] = ("dv", P.cur("dv"))
                sp_dma(Qr[64:128, a * 1024:(a + 1) * 1024], rr["t2"], waits=wr)
                REL["rope"] = ("ld", P.cur("ld"))
                REL["ar5"] = ("ld", P.cur("ld"))
                REL["ar6"] = ("ld", P.cur("ld"))
                REL[f"wh{si}"] = ("pe", P.cur("pe"))
            st["q_ready"] = [("ld", P.cur("ld"))] + edge("dv")
            if h2 % 4 == 0:
                gather_k(Kg0, kg.ap(), (kvh2 * 2 + 0) * 128, [(ccK, 1)] + rel("ar0"))
                gather_k(Kg1, kg.ap(), (kvh2 * 2 + 1) * 128, [(ccK, 1)] + rel("ar1"))
                kvst["kv_ld0"] = [("ld", P.cur("ld"))]
                gather_v(Vg_s, vg.ap(), kvh2 * 128, [(ccV, 1)] + rel("ar2"))
                kvst["kv_ld"] = [("ld", P.cur("ld"))]
                knorm_chain(Kg0, kvst["kv_ld0"], sqk, 64)
                knorm_chain(Kg1, kvst["kv_ld0"], sqkb, 80)
                kvst["sqk_dv"] = edge("dv")
            qnorm_chain(Qr[:, 0:1024], 1024, list(range(8)), st["q_ready"])
            qnorm_chain(Qr[:, 1024:2048], 1024, list(range(8, 16)), st["q_ready"])
            st["qns_ac"] = edge("ac")
            return st

        st_h = attn_qstage(0)
        for h_ in range(16):
            q_ready = st_h["q_ready"]
            qns_ac = st_h["qns_ac"]
            kv_ld0 = kvst["kv_ld0"]
            kv_ld = kvst["kv_ld"]
            sqk_dv = kvst["sqk_dv"]
            dv(lambda h: h.tensor_scalar(out=btpA[:, :], in0=qns[:, 0:8],
                                         scalar1=sqk[:, :], scalar2=-RSQD,
                                         op0=ALU.mult, op1=ALU.mult),
               waits=qns_ac + list(sqk_dv) + rel("btpa"))
            dv(lambda h: h.tensor_scalar(out=btpB[:, :], in0=qns[:, 8:16],
                                         scalar1=sqkb[:, :], scalar2=-RSQD,
                                         op0=ALU.mult, op1=ALU.mult))
            bt_dv = edge("dv")
            stA = {}
            for j in range(8):
                a1 = attn_stageA1(j, Qr, Kg0, Kg1, kv_ld0, q_ready, bt_dv, h_)
                if j > 0:
                    attn_stageB(j - 1, stA[j - 1], Vg_s, h_, kv_ld)
                stA[j] = attn_stageA2(j, a1, h_)
            REL["ar3"] = ("pe", P.cur("pe"))
            attn_stageB(7, stA[7], Vg_s, h_, kv_ld)
            REL["btpa"] = ("ac", P.cur("ac"))
            if h_ % 4 == 3:
                REL["ar0"] = ("pe", P.cur("pe"))
                REL["ar1"] = ("pe", P.cur("pe"))
                REL["ar2"] = ("pe", P.cur("pe"))
            if h_ + 1 < 16:
                st_h = attn_qstage(h_ + 1)
        attf_dv = [("dv", P.cur("dv"))]
        resid_tok = [None] * 8
        for cb in range(4):
            thr0 = wunit("wb", 0, WB[0][:, :].rearrange("p (c f) -> p c f", c=8),
                         wo_r[:, 0:8, cb * 512:(cb + 1) * 512])
            thr1 = wunit("wb", 1, WB[1][:, :].rearrange("p (c f) -> p c f", c=8),
                         wo_r[:, 8:16, cb * 512:(cb + 1) * 512])
            for j in range(8):
                for h_ in range(16):
                    si = h_ // 8
                    lhs = attf[:, h_ * 1024 + j * 128: h_ * 1024 + (j + 1) * 128]
                    rhsap = WB[si][:, (h_ % 8) * 512:((h_ % 8) + 1) * 512]
                    wt = ()
                    if h_ == 0:
                        wt = [("wd", thr0)] + attf_dv + rel("mm0")
                    elif h_ == 8:
                        wt = [("wd", thr1)]
                    pe(lambda h, lhs=lhs, rhsap=rhsap, h_=h_: h.matmul(
                        out=mm0[:, 0:512], lhsT=lhs, rhs=rhsap,
                        start=(h_ == 0), stop=(h_ == 15)), waits=wt,
                        inc=1 if h_ == 15 else 0)
                w2 = edge("pe")
                xt = xs[j][:, cb * 512:(cb + 1) * 512]
                dv(lambda h, xt=xt: h.tensor_tensor(out=xt, in0=mm0[:, 0:512], in1=xt,
                                                    op=ALU.add), waits=w2)
                REL["mm0"] = ("dv", P.cur("dv"))
                if cb == 3:
                    resid_tok[j] = ("dv", P.cur("dv"))
            REL["wb0"] = ("pe", P.cur("pe"))
            REL["wb1"] = ("pe", P.cur("pe"))
        REL["attf_w"] = ("pe", P.cur("pe"))
        REL["lnf_w"] = ("pe", P.cur("pe"))
        return resid_tok

    dep = x_ld
    if N_SUB >= 1:
        dep = build_ffn(0, dep)
    if N_SUB >= 2:
        dep = build_attn(dep)
    if N_SUB >= 3:
        dep = build_ffn(2, dep)

    for j in range(8):
        P.emit("sp", lambda h, j=j: h.dma_start(out=y_out[j * 128:(j + 1) * 128, :],
                                                in_=xs[j][:, :]),
               waits=[dep[j]], inc=[("st", 16)])
    P.emit("sp", None, waits=[("st", P.cur("st"))])

    with nc.Block() as blk:
        @blk.tensor
        def _(t):
            for f in P.ops["pe"]:
                f(t)

        @blk.vector
        def _(v):
            for f in P.ops["dv"]:
                f(v)

        @blk.scalar
        def _(s):
            for f in P.ops["ac"]:
                f(s)

        @blk.gpsimd
        def _(g):
            for f in P.ops["gp"]:
                f(g)

        @blk.sync
        def _(s):
            for f in P.ops["sp"]:
                f(s)

    return nc


_NC_CACHE = []


def _rope_perm():
    idx = []
    for h_ in range(16):
        for a in range(2):
            base = h_ * 256 + a * 128
            idx += [base + 2 * j for j in range(64)]
            idx += [base + 2 * j + 1 for j in range(64)]
    return np.array(idx)


def _rope_kperm():
    idx = []
    for kvh in range(4):
        for a in range(2):
            base = kvh * 256 + a * 128
            idx += [base + 2 * j for j in range(64)]
            idx += [base + 2 * j + 1 for j in range(64)]
    return np.array(idx)


def _host_tables(srank):
    blocks = SET[srank]
    pos = np.concatenate([np.arange(b * 128, (b + 1) * 128) for b in blocks]).astype(np.float32)
    theta = (1.0 / (10000.0 ** (np.arange(0, 256, 2, dtype=np.float32) / 256.0)))
    ang = pos[None, :] * theta[:, None]
    cos = np.cos(ang).astype(np.float32)
    sin = np.sin(ang).astype(np.float32)
    bf = ml_dtypes.bfloat16
    cs = [cos[0:64].astype(bf), sin[0:64].astype(bf),
          cos[64:128].astype(bf), sin[64:128].astype(bf)]
    m = np.zeros((8, 128, 256), dtype=np.float32)
    for j in range(8):
        i = blocks[j]
        Lp = LPAD[j] * 128
        kglob = Lp - 256 + np.arange(256)
        qglob = i * 128 + np.arange(128)
        m[j] = np.where(kglob[None, :] > qglob[:, None], NEG, 0.0)
    maskS = m.reshape(1024, 256).astype(bf)
    return cs, maskS


def kernel(L0_attn_x, L0_attn_Wq, L0_attn_Wk, L0_attn_Wv, L0_attn_Wo,
           L0_attn_lq1, L0_attn_lq2, L0_attn_lk1, L0_attn_lk2, L0_attn_rms_w,
           L0_ffn_Wa, L0_ffn_Wp, L0_ffn_w_std, L0_ffn_w_rec):
    x = np.ascontiguousarray(np.asarray(L0_attn_x, dtype=np.float32))
    wq = np.ascontiguousarray(np.asarray(L0_attn_Wq, dtype=np.float32)[:, _rope_perm()])
    wk = np.ascontiguousarray(np.asarray(L0_attn_Wk, dtype=np.float32)[:, _rope_kperm()])
    wv = np.ascontiguousarray(np.asarray(L0_attn_Wv, dtype=np.float32))
    wo = np.ascontiguousarray(np.asarray(L0_attn_Wo, dtype=np.float32))
    wa = np.ascontiguousarray(np.asarray(L0_ffn_Wa, dtype=np.float32))
    wp = np.ascontiguousarray(np.asarray(L0_ffn_Wp, dtype=np.float32))
    wstd = np.asarray(L0_ffn_w_std, dtype=np.float32).reshape(16, 1)
    wrec = np.asarray(L0_ffn_w_rec, dtype=np.float32).reshape(16, 1)
    rmsw = np.asarray(L0_attn_rms_w, dtype=np.float32).reshape(128, 1)
    lq1 = np.asarray(L0_attn_lq1, dtype=np.float32)
    lq2 = np.asarray(L0_attn_lq2, dtype=np.float32)
    lk1 = np.asarray(L0_attn_lk1, dtype=np.float32)
    lk2 = np.asarray(L0_attn_lk2, dtype=np.float32)

    bf = ml_dtypes.bfloat16
    ident = np.eye(128, dtype=bf)
    sel16 = np.zeros((16, 128), dtype=np.float32)
    for hh in range(16):
        sel16[hh, hh * 8:(hh + 1) * 8] = 1.0
    sel16 = sel16.astype(bf)

    if not _NC_CACHE:
        _NC_CACHE.append(_build_program())
    nc = _NC_CACHE[0]

    in_maps = []
    for c in range(8):
        b, s_ = c // 2, c % 2
        blocks = SET[s_]
        xl = np.concatenate([x[b, bl * 128:(bl + 1) * 128, :] for bl in blocks], axis=0)
        cs, maskS = _host_tables(s_)
        in_maps.append({
            "x": np.ascontiguousarray(xl),
            "wa": wa, "wp": wp, "wq": wq, "wk": wk, "wv": wv, "wo": wo,
            "wstd": wstd, "wrec": wrec, "rmsw": rmsw,
            "lq1": lq1, "lq2": lq2, "lk1": lk1, "lk2": lk2,
            "ident": ident, "maskS": maskS, "sel16": sel16,
            "cos0": cs[0], "sin0": cs[1], "cos1": cs[2], "sin1": cs[3],
        })
    res = run_bass_kernel_spmd(nc, in_maps, core_ids=list(range(8)))
    out = np.empty((4, 2048, 2048), dtype=np.float32)
    for c in range(8):
        b, s_ = c // 2, c % 2
        y = np.asarray(res.results[c]["y"])
        for j, bl in enumerate(SET[s_]):
            out[b, bl * 128:(bl + 1) * 128, :] = y[j * 128:(j + 1) * 128, :]
    return out
